# version 10
# baseline (speedup 1.0000x reference)
import numpy as np
import concourse.bass as bass
import concourse.mybir as mybir
from concourse.bass_utils import run_bass_kernel_spmd

F32 = mybir.dt.float32
BF16 = mybir.dt.bfloat16
AF = mybir.ActivationFunctionType
ALU = mybir.AluOpType

P = 128
D = 1024
C = 8
NPAIR = 8
ENGINES = ("tensor", "vector", "scalar", "gpsimd", "sync")


class Sem:
    __slots__ = ("h", "n")


class B:
    def __init__(self, nc):
        self.nc = nc
        self.q = {e: [] for e in ENGINES}
        self.hw = {e: {} for e in ENGINES}

    def sem(self, name):
        s = Sem()
        s.h = self.nc.alloc_semaphore(name)
        s.n = 0
        return s

    def wait(self, eng, sem, n=None):
        v = sem.n if n is None else n
        if v <= 0:
            return
        if v <= self.hw[eng].get(id(sem), 0):
            return
        self.hw[eng][id(sem)] = v
        self.q[eng].append(lambda e, _s=sem.h, _v=v: e.wait_ge(_s, _v))

    def op(self, eng, fn, inc=None, by=1):
        if inc is None:
            self.q[eng].append(fn)
            return None
        inc.n += by

        def g(e, _fn=fn, _s=inc.h, _b=by):
            _fn(e).then_inc(_s, _b)

        self.q[eng].append(g)
        return inc.n

    def emit(self):
        with self.nc.Block() as blk:
            @blk.tensor
            def _(t):
                for f in self.q["tensor"]:
                    f(t)

            @blk.vector
            def _(v):
                for f in self.q["vector"]:
                    f(v)

            @blk.scalar
            def _(s):
                for f in self.q["scalar"]:
                    f(s)

            @blk.gpsimd
            def _(g):
                for f in self.q["gpsimd"]:
                    f(g)

            @blk.sync
            def _(sy):
                for f in self.q["sync"]:
                    f(sy)


class Busy:
    def __init__(self, b):
        self.b = b
        self.edges = []

    def set(self, sem, n=None):
        self.edges = [(sem, sem.n if n is None else n)]

    def wait(self, eng):
        for (s, n) in self.edges:
            self.b.wait(eng, s, n)


def build_nc(T=2048, upto=4):
    NT = T // 512
    NQ = T // 128
    NG = T // 512
    HTOK = min(T, 1024)
    NHALF = T // HTOK
    NTH = HTOK // 512
    nc = bass.Bass("TRN2", target_bir_lowering=False, debug=False)

    def din(name, shape, dt=F32):
        return nc.dram_tensor(name, list(shape), dt, kind="ExternalInput").ap()

    xt = din("xt", (D, T))
    wqkv0g = din("wqkv0g", (D, NPAIR, 384))
    wproj0T = din("wproj0T", (D, D))
    wrT = din("wrT", (D, D))
    wkT = din("wkT", (D, D))
    wvT = din("wvT", (D, D))
    murD = din("mur", (P, C))
    mukD = din("muk", (P, C))
    wqkv1g = din("wqkv1g", (D, NPAIR, 384))
    wpre1 = din("wpre1", (D, D))
    wproj1 = din("wproj1", (D, D))
    bpreD = din("bpre", (P, C))
    bprojD = din("bproj", (P, C))
    piD = din("pi", (1, 1))
    w1T = din("w1T", (D, 4096))
    w2T = din("w2T", (D, 4096))
    h1T = din("h1T", (4096, 4096))
    h2T = din("h2T", (4096, 4096))
    w3T = din("w3T", (4096, D))
    s1D = din("s1", (P, 32))
    s2D = din("s2", (P, 32))
    sh1D = din("sh1", (P, 32))
    sh2D = din("sh2", (P, 32))
    s3D = din("s3", (P, C))
    ones32D = din("ones32", (1, P))
    epsvD = din("epsv", (1, 1))
    onesbD = din("onesb", (P, 1))
    identD = din("ident", (P, P))
    maskD = din("maskc", (P, P))

    y = nc.dram_tensor("y", [P, C, T], F32, kind="ExternalOutput").ap()
    dbg1 = nc.dram_tensor("dbg1", [P, C, T], F32, kind="ExternalOutput").ap()
    dbg2 = nc.dram_tensor("dbg2", [P, C, T], F32, kind="ExternalOutput").ap()
    rsp1 = nc.dram_tensor("rsp1", [P, C, T], F32, kind="Internal").ap()
    rsp2 = nc.dram_tensor("rsp2", [P, C, T], F32, kind="Internal").ap()

    c_ones32 = nc.alloc_sbuf_tensor("c_ones32", [1, P], F32)
    c_onesb = nc.alloc_sbuf_tensor("c_onesb", [P, 1], BF16)
    c_ident = nc.alloc_sbuf_tensor("c_ident", [P, P], BF16)
    c_mask = nc.alloc_sbuf_tensor("c_mask", [P, P], BF16)
    c_mur = nc.alloc_sbuf_tensor("c_mur", [P, C], F32)
    c_muk = nc.alloc_sbuf_tensor("c_muk", [P, C], F32)
    c_bpre = nc.alloc_sbuf_tensor("c_bpre", [P, C], F32)
    c_bproj = nc.alloc_sbuf_tensor("c_bproj", [P, C], F32)
    c_s1 = nc.alloc_sbuf_tensor("c_s1", [P, 32], F32)
    c_s2 = nc.alloc_sbuf_tensor("c_s2", [P, 32], F32)
    c_sh1 = nc.alloc_sbuf_tensor("c_sh1", [P, 32], F32)
    c_sh2 = nc.alloc_sbuf_tensor("c_sh2", [P, 32], F32)
    c_s3 = nc.alloc_sbuf_tensor("c_s3", [P, C], F32)
    c_pi = nc.alloc_sbuf_tensor("c_pi", [1, 1], F32)
    c_eps = nc.alloc_sbuf_tensor("c_eps", [1, 1], F32)
    pi32 = nc.alloc_sbuf_tensor("pi32", [P, 1], F32)
    statv = nc.alloc_sbuf_tensor("statv", [P, 16], F32)

    Yt = nc.alloc_sbuf_tensor("Yt", [P, C * (T + 1)], BF16)
    Y = Yt[:].rearrange("p (c t) -> p c t", c=C)

    ring_t = nc.alloc_sbuf_tensor("ring", [P, 6144], BF16)

    SA_BYTES = 32768
    arenaS_t = nc.alloc_sbuf_tensor("arenaS", [P, SA_BYTES // 4], F32)
    AD_BYTES = max(C * T * 4 + 6 * T + C * T * 2,
                   6 * C * T + 16384 + 4 * T + 8192,
                   128 * HTOK)
    arenaD_t = nc.alloc_sbuf_tensor("arenaD", [P, AD_BYTES // 4], F32)

    def view(arena_t, byte_off, shape, dt):
        elt = 2 if dt == BF16 else 4
        assert byte_off % elt == 0
        ncols = 1
        for s in shape[1:]:
            ncols *= s
        hh = arena_t.bitcast(dt) if dt == BF16 else arena_t
        ap = hh[:][:, byte_off // elt: byte_off // elt + ncols]
        if len(shape) == 3:
            ap = ap.rearrange("p (a b) -> p a b", a=shape[1])
        elif len(shape) == 4:
            ap = ap.rearrange("p (a b c) -> p a b c", a=shape[1], b=shape[2])
        return ap

    t1_v = view(arenaS_t, 0, [P, 512], F32)
    t2_v = view(arenaS_t, 2048, [P, 2, 512], F32)
    rb_v = view(arenaS_t, 6144, [P, 2, 512], BF16)
    rsq_v = view(arenaS_t, 8192, [P, 2, 512], BF16)
    st_mean = view(arenaS_t, 10240, [P, 512], F32)[0:1, :]
    st_tmp = view(arenaS_t, 12288, [P, 512], F32)[0:1, :]
    st_rstd = view(arenaS_t, 14336, [P, 512], F32)[0:1, :]
    d_bf = view(arenaS_t, 16384, [P, 512], BF16)
    S_sb = view(arenaS_t, 0, [P, 2, T], F32)
    Pb = view(arenaS_t, 8 * T, [P, 2, T], BF16)
    PT_sb = view(arenaS_t, 12 * T, [P, 2, T // 128, P], BF16)
    ksb_v = view(arenaS_t, 0, [P, 2, 512], F32)
    tv_v = view(arenaS_t, 4096, [P, 2, 512], F32)
    sa_v = view(arenaS_t, 17408, [P, 2, HTOK], BF16)
    rw3_v = view(arenaS_t, 17408 + 4 * HTOK, [P, 2, HTOK], F32)

    R = view(arenaD_t, 0, [P, C, T], F32)
    a0 = C * T * 4
    Qp = view(arenaD_t, a0 + 0, [P, T], BF16)
    Kp = view(arenaD_t, a0 + 2 * T, [P, T], BF16)
    Vf0 = view(arenaD_t, a0 + 4 * T, [P, T], BF16)
    VT0 = view(arenaD_t, a0 + 4 * T, [P, T // 128, P], BF16)
    O_all = view(arenaD_t, a0 + 6 * T, [P, C, T], BF16)
    xr_v = view(arenaD_t, a0 + 0, [P, C, 512], BF16)
    xk_v = view(arenaD_t, a0 + C * 1024, [P, C, 512], BF16)
    r_all = view(arenaD_t, a0 + C * 2048, [P, C, 512], F32)
    kact_v = view(arenaD_t, a0 + C * 4096, [P, C, 512], BF16)
    pre_v = view(arenaD_t, 0, [P, C, T], BF16)
    K_all = view(arenaD_t, 2 * C * T, [P, C, T], BF16)
    VT_all = view(arenaD_t, 4 * C * T, [P, C, T // 128, P], BF16)
    bB = 6 * C * T
    VfB = view(arenaD_t, bB, [P, T], BF16)
    Qg_v = view(arenaD_t, bB, [P, 2, C, 512], BF16)
    Oq_v = view(arenaD_t, bB + 16384, [P, 2, C, 128], BF16)
    lg_v = view(arenaD_t, bB + 20480, [P, 2, T], BF16)
    rbuf = view(arenaD_t, bB + 20480 + 4 * T, [P, 2, 128], F32)
    Rt512 = view(arenaD_t, 0, [P, 2, C, 512], F32)
    h_v = view(arenaD_t, 0, [P, 32, HTOK], BF16)
    h1g_v = view(arenaD_t, 64 * HTOK, [P, 32, HTOK], BF16)
    hg2_v = h_v

    big = nc.alloc_psum_tensor("big", [P, 2048], F32)[:]
    tp = nc.alloc_psum_tensor("tp", [P, 1024], BF16)[:]
    pv = nc.alloc_psum_tensor("pv", [P, 512], F32)[:]
    qp_h = nc.alloc_psum_tensor("qp", [P, 512], F32)
    qp = qp_h[:]
    qp_bf = qp_h.bitcast(BF16)[:]
    pj = nc.alloc_psum_tensor("pj", [P, 512], F32)[:]

    b = B(nc)
    sSY = b.sem("sSY")
    sGP = b.sem("sGP")
    sW = [b.sem(f"sW{i}") for i in range(3)]
    sRB = [b.sem("sRB0"), b.sem("sRB1")]
    sRW = [b.sem("sRW0"), b.sem("sRW1")]
    sRT = [b.sem("sRT0"), b.sem("sRT1")]
    sSP = b.sem("sSP")
    sTE = b.sem("sTE")
    sDV = b.sem("sDV")
    sAC = b.sem("sAC")
    sPL = b.sem("sPL")

    bz_big = Busy(b)
    bz_bigLo = Busy(b)
    bz_bigHi = Busy(b)
    bz_bigA = Busy(b)
    bz_bigB = Busy(b)
    bz_tp = Busy(b)
    bz_pv = Busy(b)
    bz_qp = Busy(b)
    bz_pj = Busy(b)

    def mm(out, lh, rh, first, last, inc=None):
        return b.op("tensor",
                    lambda t, _o=out, _l=lh, _r=rh, _f=first, _s=last:
                    t.matmul(_o, _l, _r, start=_f, stop=_s), inc=inc)

    def gdma(out, in_, wait_list=()):
        for (s, n) in wait_list:
            b.wait("gpsimd", s, n)
        return b.op("gpsimd", lambda g, _o=out, _i=in_: g.dma_start(out=_o, in_=_i),
                    inc=sGP, by=16)

    def sdma(out, in_, wait_list=()):
        for (s, n) in wait_list:
            b.wait("sync", s, n)
        return b.op("sync", lambda sy, _o=out, _i=in_: sy.dma_start(out=_o, in_=_i),
                    inc=sSY, by=16)

    NSLOT = 3
    slotA = [view(ring_t, i * 4096, [P, 8, 256], BF16) for i in range(NSLOT)]
    slot_free = [[(sTE, 0)] for _ in range(NSLOT)]
    slot_loaded = [0] * NSLOT
    slot_rr = [0]

    def load_slot(src_ap, width=256, dst4=False):
        i = slot_rr[0] % NSLOT
        slot_rr[0] += 1
        if isinstance(src_ap, list):
            n = None
            for j, sap in enumerate(src_ap):
                for (s_, n_) in (slot_free[i] if j == 0 else ()):
                    b.wait("gpsimd", s_, n_)
                n = b.op("gpsimd",
                         lambda g, _o=slotA[i][:, :, 128 * j: 128 * j + 128],
                         _i=sap: g.dma_start(out=_o, in_=_i), inc=sW[i], by=16)
            slot_loaded[i] = n
            return i
        dst = slotA[i] if width == 256 else slotA[i][:, :, 0:width]
        for (s_, n_) in slot_free[i]:
            b.wait("gpsimd", s_, n_)
        n = b.op("gpsimd", lambda g, _o=dst, _i=src_ap:
                 g.dma_start(out=_o, in_=_i), inc=sW[i], by=16)
        slot_loaded[i] = n
        return i

    def use_slot(i):
        b.wait("tensor", sW[i], slot_loaded[i])

    for dst, src in ((c_ones32, ones32D), (c_mur, murD), (c_muk, mukD),
                     (c_bpre, bpreD), (c_bproj, bprojD), (c_s1, s1D), (c_s2, s2D),
                     (c_sh1, sh1D), (c_sh2, sh2D), (c_s3, s3D), (c_pi, piD),
                     (c_eps, epsvD)):
        sdma(dst[:], src)
    n_const_sy = sSY.n
    gdma(c_onesb[:], onesbD)
    gdma(c_ident[:], identD)
    gdma(c_mask[:], maskD)
    n_const_gp = sGP.n
    ld_x = sdma(R, xt.rearrange("(c p) t -> p c t", p=P))
    b.wait("tensor", sSY, n_const_sy)
    mm(qp[:, 0:1], c_ones32[:], c_pi[:], True, True, inc=sTE)
    b.wait("vector", sTE)
    b.op("vector", lambda v: v.tensor_scalar(pi32[:], qp[:, 0:1], 1.0 / 32.0, None,
                                             ALU.mult), inc=sDV)
    bz_qp.set(sDV)
    b.op("vector", lambda v: v.memset(Y[:, :, 0:1], 0.0), inc=sDV)

    def layer_norm(src3, update_R=False, src_ready=None, src_wait=()):
        for s_ in src_wait:
            b.wait("vector", s_)
            b.wait("scalar", s_)
        t2free = [(sPL, 0), (sPL, 0)]
        for tt in range(NT):
            sl = slice(512 * tt, 512 * tt + 512)
            if src_ready is not None:
                src_ready(tt)
            bz_qp.wait("tensor")
            bz_pv.wait("tensor")
            te_c = [0, 0]
            for c in range(C):
                par = c & 1
                if te_c[par]:
                    b.wait("vector", sTE, te_c[par])
                    b.wait("scalar", sTE, te_c[par])
                dv = b.op("vector", lambda v, _c=c, _p=par, _sl=sl:
                          v.tensor_copy(rb_v[:, _p, :], src3(_c, _sl)), inc=sDV)
                ac = b.op("scalar", lambda s, _c=c, _p=par, _sl=sl:
                          s.square(rsq_v[:, _p, :], src3(_c, _sl)), inc=sAC)
                b.wait("tensor", sDV, dv)
                b.wait("tensor", sAC, ac)
                mm(qp[0:1, 0:512], c_onesb[:], rb_v[:, par, :], c == 0, c == C - 1)
                te_c[par] = mm(pv[0:1, 0:512], c_onesb[:], rsq_v[:, par, :],
                               c == 0, c == C - 1, inc=sTE)
            te_stats = sTE.n
            b.wait("vector", sTE, te_stats)
            dv_mean = b.op("vector", lambda v: v.tensor_scalar(
                st_mean, qp[0:1, 0:512], 1.0 / 1024.0, None, ALU.mult), inc=sDV)
            b.wait("scalar", sDV, dv_mean)
            ac_m2 = b.op("scalar", lambda s: s.square(st_tmp, st_mean), inc=sAC)
            b.wait("vector", sAC, ac_m2)
            dv_var = b.op("vector", lambda v: v.scalar_tensor_tensor(
                st_tmp, pv[0:1, 0:512], 1.0 / 1024.0, st_tmp,
                ALU.mult, ALU.subtract), inc=sDV)
            b.wait("scalar", sDV, dv_var)
            ac_sd = b.op("scalar", lambda s: s.activation(
                st_tmp, st_tmp, AF.Sqrt, bias=c_eps[:], scale=1.0), inc=sAC)
            b.wait("vector", sAC, ac_sd)
            dv_rstd = b.op("vector", lambda v: v.reciprocal(st_rstd, st_tmp), inc=sDV)
            bz_qp.set(sDV, dv_mean)
            bz_pv.set(sDV, dv_var)
            b.wait("tensor", sDV, dv_rstd)
            bz_big.wait("tensor")
            bz_bigLo.wait("tensor")
            bz_bigHi.wait("tensor")
            mm(big[:, 0:512], c_ones32[:], st_mean, True, True)
            te_bc = mm(big[:, 512:1024], c_ones32[:], st_rstd, True, True, inc=sTE)
            b.wait("vector", sTE, te_bc)
            last_dv = None
            for c in range(C):
                b.op("vector", lambda v, _c=c, _sl=sl: v.tensor_tensor(
                    t1_v, src3(_c, _sl), big[:, 0:512], ALU.subtract))
                if update_R:
                    par = c & 1
                    b.wait("vector", *t2free[par])
                    dvn = b.op("vector", lambda v, _p=par: v.tensor_tensor(
                        t2_v[:, _p, :], t1_v, big[:, 512:1024], ALU.mult), inc=sDV)
                    b.wait("gpsimd", sDV, dvn)
                    b.op("gpsimd", lambda g, _c=c, _p=par, _sl=sl: g.tensor_copy(
                        Y[:, _c, 1 + _sl.start: 1 + _sl.stop], t2_v[:, _p, :]))
                    pl = b.op("gpsimd", lambda g, _c=c, _p=par, _sl=sl: g.tensor_tensor(
                        R[:, _c, _sl], R[:, _c, _sl], t2_v[:, _p, :], ALU.add), inc=sPL)
                    t2free[par] = (sPL, pl)
                    last_dv = dvn
                else:
                    last_dv = b.op("vector", lambda v, _c=c, _sl=sl: v.tensor_tensor(
                        Y[:, _c, 1 + _sl.start: 1 + _sl.stop], t1_v,
                        big[:, 512:1024], ALU.mult), inc=sDV)
            if update_R:
                bz_big.set(sPL)
                bz_bigLo.set(sPL)
            else:
                bz_big.set(sDV, last_dv)
                bz_bigLo.set(sDV, last_dv)

    def R_src(c, sl):
        return R[:, c, sl]

    b.wait("vector", sSY, ld_x)
    b.wait("scalar", sSY, ld_x)
    b.wait("gpsimd", sGP, n_const_gp)
    if upto == 0:
        b.wait("sync", sSY, ld_x)
        sdma(y, R)
        b.wait("sync", sSY)
        b.emit()
        return nc
    layer_norm(R_src)
    if upto == 10:
        b.wait("gpsimd", sDV)
        b.wait("gpsimd", sPL)
        gdma(y, Y[:, :, 1:T + 1])
        b.wait("gpsimd", sGP)
        b.emit()
        return nc

    def wstream(wsrc, oc_list, tiles, rhs_fn, evac, N=512, wwidth=256, dst4=False):
        si = None
        nexts = list(oc_list)
        si = load_slot(wsrc(nexts[0]), wwidth, dst4)
        for i, oce in enumerate(nexts):
            nsi = (load_slot(wsrc(nexts[i + 1]), wwidth, dst4)
                   if i + 1 < len(nexts) else None)
            use_slot(si)
            te = None
            for sub in ([0, 128] if wwidth == 256 else [0]):
                oc = oce if wwidth == 128 else oce * 2 + (sub // 128)
                for ti in tiles:
                    ppsel = (qp, bz_qp) if (oc & 1) == 0 else (pj, bz_pj)
                    pp, bz = ppsel
                    bz.wait("tensor")
                    for kc in range(C):
                        te = mm(pp[:, 0:N], slotA[si][:, kc, sub: sub + 128],
                                rhs_fn(kc, ti), kc == 0, kc == C - 1,
                                inc=(sTE if kc == C - 1 else None))
                    edge = evac(oc, ti, pp, te)
                    bz.edges = [edge]
            slot_free[si] = [(sTE, te)]
            si = nsi

    def attn_score_softmax(it, qt, Q_ap, K_ap, logits_par, causal, fe):
        par = it & 1
        nk = qt + 1 if causal else NQ
        kl = nk * 128
        nkt = (kl + 511) // 512
        te_s = None
        te_lo = None
        for kt in range(nkt):
            if kt < 2:
                bz_bigLo.wait("tensor")
            else:
                bz_bigHi.wait("tensor")
            n = min(512, kl - kt * 512)
            ksl = slice(kt * 512, kt * 512 + n)
            diag_here = causal and (qt * 128 >= kt * 512) and (qt * 128 < kt * 512 + n)
            lg_here = logits_par is not None
            last_plain = not (diag_here or lg_here)
            lo_end = (kt == 1 or kt == nkt - 1) and kl > 1024 and kt < 2
            te_s = mm(big[:, ksl], Q_ap, K_ap[:, ksl], True, last_plain,
                      inc=(sTE if (last_plain and (kt == nkt - 1 or lo_end))
                           else None))
            if lg_here:
                te_s = mm(big[:, ksl], c_ident[:], lg_v[:, logits_par, ksl],
                          False, not diag_here,
                          inc=(sTE if ((not diag_here) and (kt == nkt - 1 or lo_end))
                               else None))
            if diag_here:
                dsl = slice(qt * 128, qt * 128 + 128)
                te_s = mm(big[:, dsl], c_ident[:], c_mask[:], False, True,
                          inc=(sTE if (kt == nkt - 1 or lo_end) else None))
            if lo_end:
                te_lo = te_s
        base = par * 8
        mxA = statv[:, base: base + 1]
        mxB = statv[:, base + 1: base + 2]
        nm = statv[:, base + 2: base + 3]
        zz = statv[:, base + 3: base + 4]
        rz = statv[:, base + 4: base + 5]
        b.wait("vector", *fe["Sfree"][par])
        if kl > 1024:
            b.wait("vector", sTE, te_lo)
            dv_lo = b.op("vector", lambda v, _p=par, _m=mxA: v.tensor_scalar(
                S_sb[:, _p, 0:1024], big[:, 0:1024], 0.0, None, ALU.add, ALU.max,
                accum_out=_m), inc=sDV)
            bz_bigLo.set(sDV, dv_lo)
            b.wait("vector", sTE, te_s)
            dv_evac = b.op("vector", lambda v, _p=par, _k=kl, _m=mxB: v.tensor_scalar(
                S_sb[:, _p, 1024:_k], big[:, 1024:_k], 0.0, None, ALU.add, ALU.max,
                accum_out=_m), inc=sDV)
            bz_bigHi.set(sDV, dv_evac)
            b.wait("vector", sDV, dv_evac)
            b.op("vector", lambda v, _a=mxA, _b2=mxB: v.tensor_max(_a, _a, _b2),
                 inc=sDV)
            b.wait("vector", sDV)
            dv_nm = b.op("vector", lambda v, _m=mxA, _n=nm: v.tensor_scalar(
                _n, _m, -1.0, None, ALU.mult), inc=sDV)
        else:
            b.wait("vector", sTE, te_s)
            dv_evac = b.op("vector", lambda v, _p=par, _k=kl, _m=mxA: v.tensor_scalar(
                S_sb[:, _p, 0:_k], big[:, 0:_k], 0.0, None, ALU.add, ALU.max,
                accum_out=_m), inc=sDV)
            bz_bigLo.set(sDV, dv_evac)
            bz_bigHi.set(sDV, dv_evac)
            b.wait("vector", sDV, dv_evac)
            dv_nm = b.op("vector", lambda v, _m=mxA, _n=nm: v.tensor_scalar(
                _n, _m, -1.0, None, ALU.mult), inc=sDV)
        b.wait("scalar", sDV, dv_nm)
        b.wait("scalar", *fe["Pfree"][par])
        ac_exp = b.op("scalar", lambda s, _p=par, _k=kl, _n=nm, _z=zz: s.activation(
            Pb[:, _p, 0:_k], S_sb[:, _p, 0:_k], AF.Exp, bias=_n, scale=1.0,
            accum_out=_z), inc=sAC)
        fe["Sfree"][par] = (sAC, ac_exp)
        return ac_exp

    def attn_pv(it, qt, h, ac_exp, VT_ap, O_dst, causal, fe):
        par = it & 1
        nk = qt + 1 if causal else NQ
        zz = statv[:, par * 8 + 3: par * 8 + 4]
        rz = statv[:, par * 8 + 4: par * 8 + 5]
        b.wait("vector", sAC, ac_exp)
        dv_rz = b.op("vector", lambda v, _z=zz, _r=rz: v.reciprocal(_r, _z), inc=sDV)
        b.wait("vector", sDV, dv_rz)
        kl = nk * 128
        dv_norm = b.op("vector", lambda v, _p=par, _k=kl, _r=rz: v.tensor_scalar(
            Pb[:, _p, 0:_k], Pb[:, _p, 0:_k], _r, None, ALU.mult), inc=sDV)
        b.wait("tensor", sDV, dv_norm)
        pt_edges = []
        for w in range((nk + 7) // 8):
            wn = min(8, nk - w * 8)
            wi = wave_ctr[0] & 1
            wave_ctr[0] += 1
            bank = tp if wi == 0 else qp_bf
            bz_bank = bz_tp if wi == 0 else bz_qp
            bz_bank.wait("tensor")
            te_w = None
            for j in range(wn):
                kc = w * 8 + j
                te_w = b.op("tensor", lambda t, _j=j, _kc=kc, _p=par, _b=bank:
                            t.transpose(_b[:, _j * 128: _j * 128 + 128],
                                        Pb[:, _p, _kc * 128: _kc * 128 + 128],
                                        c_ident[:]),
                            inc=(sTE if j == wn - 1 else None))
            eng = "vector" if wi == 0 else "scalar"
            esem = sDV if wi == 0 else sAC
            b.wait(eng, sTE, te_w)
            for (s_, n_) in [fe["PTfree"][par]]:
                b.wait(eng, s_, n_)
            if wi == 0:
                ac_pt = b.op(eng, lambda e, _w=w, _wn=wn, _p=par, _b=bank:
                             e.tensor_copy(PT_sb[:, _p, _w * 8: _w * 8 + _wn, :],
                                           _b[:, 0: _wn * 128]), inc=esem)
            else:
                ac_pt = b.op(eng, lambda e, _w=w, _wn=wn, _p=par, _b=bank:
                             e.copy(PT_sb[:, _p, _w * 8: _w * 8 + _wn, :],
                                    _b[:, 0: _wn * 128]), inc=esem)
            bz_bank.set(esem, ac_pt)
            pt_edges.append((esem, ac_pt))
        fe["Pfree"][par] = (sTE, sTE.n)
        for (s_, n_) in pt_edges:
            b.wait("tensor", s_, n_)
        if h == 0:
            bz_pv.wait("tensor")
        te_pv = None
        for kc in range(nk):
            te_pv = mm(pv[64 * h: 64 * h + 64, 0:128], VT_ap(kc),
                       PT_sb[:, par, kc, :], kc == 0, kc == nk - 1,
                       inc=(sTE if kc == nk - 1 else None))
        fe["PTfree"][par] = (sTE, te_pv)
        if h == 1:
            b.wait("vector", sTE, te_pv)
            b.wait("vector", *fe["Ofree"][par])
            dv_o = b.op("vector", lambda v, _d=O_dst: v.tensor_copy(_d, pv[:, 0:128]),
                        inc=sDV)
            bz_pv.set(sDV, dv_o)

    def fresh_fe():
        return {"Sfree": [(sDV, 0), (sDV, 0)],
                "Pfree": [(sDV, 0), (sDV, 0)],
                "PTfree": [(sDV, 0), (sDV, 0)],
                "Ofree": [(sDV, 0), (sDV, 0)]}

    def transpose_to(src_fm, dst_vt):
        for w in range((NQ + 7) // 8):
            wn = min(8, NQ - w * 8)
            bz_tp.wait("tensor")
            te_w = None
            for j in range(wn):
                tc = w * 8 + j
                te_w = b.op("tensor", lambda t, _j=j, _tc=tc: t.transpose(
                    tp[:, _j * 128: _j * 128 + 128],
                    src_fm[:, _tc * 128: _tc * 128 + 128], c_ident[:]),
                    inc=(sTE if j == wn - 1 else None))
            b.wait("scalar", sTE, te_w)
            ac_v = b.op("scalar", lambda s, _w=w, _wn=wn: s.copy(
                dst_vt[:, _w * 8: _w * 8 + _wn, :], tp[:, 0: _wn * 128]), inc=sAC)
            bz_tp.set(sAC, ac_v)

    def ysrc(kc, tt):
        return Y[:, kc, 1 + 512 * tt: 1 + 512 * tt + 512]

    wave_ctr = [0]

    def l0_attn(dump_o=False):
        fe = fresh_fe()
        it_ctr = [0]
        pend = []
        b.wait("tensor", sDV)
        for hp in range(NPAIR):
            def wsrc_qk(_e, _hp=hp):
                return wqkv0g[:, _hp, 0:256].rearrange("(c p) m -> p c m", p=P)

            def evac_qk(oc, tt, pp, te):
                b.wait("scalar", sTE, te)
                if oc % 2 == 0:
                    ac = b.op("scalar", lambda s, _tt=tt, _pp=pp: s.mul(
                        Qp[:, 512 * _tt: 512 * _tt + 512], _pp[:, 0:512], 0.125),
                        inc=sAC)
                else:
                    ac = b.op("scalar", lambda s, _tt=tt, _pp=pp: s.copy(
                        Kp[:, 512 * _tt: 512 * _tt + 512], _pp[:, 0:512]), inc=sAC)
                return (sAC, ac)

            wstream(wsrc_qk, [0], list(range(NT)), ysrc, evac_qk)

            def wsrc_v(_e, _hp=hp):
                return wqkv0g[:, _hp, 256:384].rearrange("(c p) m -> p c m", p=P)

            def evac_v(oc, tt, pp, te):
                b.wait("scalar", sTE, te)
                ac = b.op("scalar", lambda s, _tt=tt, _pp=pp: s.copy(
                    Vf0[:, 512 * _tt: 512 * _tt + 512], _pp[:, 0:512]), inc=sAC)
                return (sAC, ac)

            wstream(wsrc_v, [0], list(range(NT)), ysrc, evac_v, wwidth=128)
            b.wait("tensor", sAC)
            transpose_to(Vf0, VT0)
            b.wait("tensor", sAC)
            for qt in range(NQ):
                for h in (0, 1):
                    it = it_ctr[0]
                    it_ctr[0] += 1
                    if len(pend) == 2:
                        attn_pv(*pend.pop(0))
                    dvn = attn_score_softmax(
                        it, qt,
                        Q_ap=Qp[64 * h: 64 * h + 64, qt * 128: qt * 128 + 128],
                        K_ap=Kp[64 * h: 64 * h + 64, :],
                        logits_par=None, causal=True, fe=fe)
                    pend.append((it, qt, h, dvn,
                                 (lambda kc, _h=h: VT0[:, kc, 64 * _h: 64 * _h + 64]),
                                 O_all[:, hp, qt * 128: qt * 128 + 128], True, fe))
            while pend:
                attn_pv(*pend.pop(0))

        def wsrc_p0(ocp):
            return wproj0T[:, ocp * 256: ocp * 256 + 256].rearrange(
                "(c p) m -> p c m", p=P)

        def rhs_o(kc, tt):
            return O_all[:, kc, 512 * tt: 512 * tt + 512]

        def evac_p0(oc, tt, pp, te):
            b.wait("vector", sTE, te)
            dv = b.op("vector", lambda v, _oc=oc, _tt=tt, _pp=pp: v.tensor_tensor(
                R[:, _oc, 512 * _tt: 512 * _tt + 512], _pp[:, 0:512],
                R[:, _oc, 512 * _tt: 512 * _tt + 512], ALU.add), inc=sDV)
            return (sDV, dv)

        while pend:
            attn_pv(*pend.pop(0))
        if not dump_o:
            b.wait("tensor", sDV)
            wstream(wsrc_p0, list(range(4)), list(range(NT)), rhs_o, evac_p0)

    if upto == 11:
        fe = fresh_fe()
        b.wait("tensor", sDV)
        hp = 0
        def wsrc_qk(_e, _hp=hp):
            return wqkv0g[:, _hp, 0:256].rearrange("(c p) m -> p c m", p=P)
        def evac_qk(oc, tt, pp, te):
            b.wait("scalar", sTE, te)
            if oc % 2 == 0:
                ac = b.op("scalar", lambda s, _tt=tt, _pp=pp: s.mul(
                    Qp[:, 512 * _tt: 512 * _tt + 512], _pp[:, 0:512], 0.125),
                    inc=sAC)
            else:
                ac = b.op("scalar", lambda s, _tt=tt, _pp=pp: s.copy(
                    Kp[:, 512 * _tt: 512 * _tt + 512], _pp[:, 0:512]), inc=sAC)
            return (sAC, ac)
        wstream(wsrc_qk, [0], list(range(NT)), ysrc, evac_qk)
        def wsrc_v(_e, _hp=hp):
            return wqkv0g[:, _hp, 256:384].rearrange("(c p) m -> p c m", p=P)
        def evac_v(oc, tt, pp, te):
            b.wait("scalar", sTE, te)
            ac = b.op("scalar", lambda s, _tt=tt, _pp=pp: s.copy(
                Vf0[:, 512 * _tt: 512 * _tt + 512], _pp[:, 0:512]), inc=sAC)
            return (sAC, ac)
        wstream(wsrc_v, [0], list(range(NT)), ysrc, evac_v, wwidth=128)
        b.wait("tensor", sAC)
        transpose_to(Vf0, VT0)
        b.wait("gpsimd", sAC)
        gdma(y[:, 0, :], Qp[:, :])
        gdma(y[:, 1, :], Kp[:, :])
        gdma(y[:, 2, 0:512], VT0[:, :, :].rearrange("p a b -> p (a b)")[:, 0:512])
        b.wait("gpsimd", sGP)
        b.emit()
        return nc
    if upto == 13:
        fe = fresh_fe()
        b.wait("tensor", sDV)
        hp = 0
        def wsrc_qk(_e, _hp=hp):
            return wqkv0g[:, _hp, 0:256].rearrange("(c p) m -> p c m", p=P)
        def evac_qk(oc, tt, pp, te):
            b.wait("scalar", sTE, te)
            if oc % 2 == 0:
                ac = b.op("scalar", lambda s, _tt=tt, _pp=pp: s.mul(
                    Qp[:, 512 * _tt: 512 * _tt + 512], _pp[:, 0:512], 0.125),
                    inc=sAC)
            else:
                ac = b.op("scalar", lambda s, _tt=tt, _pp=pp: s.copy(
                    Kp[:, 512 * _tt: 512 * _tt + 512], _pp[:, 0:512]), inc=sAC)
            return (sAC, ac)
        wstream(wsrc_qk, [0], list(range(NT)), ysrc, evac_qk)
        def wsrc_v(_e, _hp=hp):
            return wqkv0g[:, _hp, 256:384].rearrange("(c p) m -> p c m", p=P)
        def evac_v(oc, tt, pp, te):
            b.wait("scalar", sTE, te)
            ac = b.op("scalar", lambda s, _tt=tt, _pp=pp: s.copy(
                Vf0[:, 512 * _tt: 512 * _tt + 512], _pp[:, 0:512]), inc=sAC)
            return (sAC, ac)
        wstream(wsrc_v, [0], list(range(NT)), ysrc, evac_v, wwidth=128)
        b.wait("tensor", sAC)
        transpose_to(Vf0, VT0)
        b.wait("tensor", sAC)
        _dvn = attn_score_softmax(1, 1, Q_ap=Qp[0:64, 128:256],
                                  K_ap=Kp[0:64, :], logits_par=None,
                                  causal=True, fe=fe)
        attn_pv(1, 1, 0, _dvn, (lambda kc: VT0[:, kc, 0:64]),
                O_all[:, 0, 128:256], True, fe)
        b.wait("vector", sTE)
        b.op("vector", lambda v: v.tensor_copy(O_all[:, 1, 0:128], pv[:, 0:128]),
             inc=sDV)
        b.wait("gpsimd", sDV)
        b.wait("gpsimd", sAC)
        gdma(y[:, 0, 0:256], S_sb[:, 1, 0:256])
        gdma(y[:, 1, 0:256], Pb[:, 1, 0:256])
        gdma(y[:, 2, 0:16], statv[:])
        gdma(y[:, 3, 0:256], PT_sb[:, 1, :, :].rearrange("p a b -> p (a b)")[:, 0:256])
        gdma(y[:, 4, 0:128], O_all[:, 1, 0:128])
        b.wait("gpsimd", sGP)
        b.emit()
        return nc
    l0_attn(dump_o=(upto == 12))
    if upto == 12:
        b.wait("gpsimd", sDV)
        gdma(y, O_all)
        b.wait("gpsimd", sGP)
        b.emit()
        return nc
    if upto == 1:
        b.wait("sync", sDV)
        sdma(y, R)
        b.wait("sync", sSY)
        b.emit()
        return nc

    layer_norm(R_src, src_wait=(sDV,))

    def l0_ffn():
        for tt in range(NT):
            sl = slice(512 * tt, 512 * tt + 512)
            dv_mix = None
            for c in range(C):
                yc = Y[:, c, 1 + sl.start: 1 + sl.stop]
                yp = Y[:, c, sl.start: sl.stop]
                b.op("vector", lambda v, _a=yc, _b2=yp: v.tensor_tensor(
                    d_bf[:, :], _a, _b2, ALU.subtract))
                b.op("vector", lambda v, _c=c, _y=yp: v.scalar_tensor_tensor(
                    xr_v[:, _c, :], d_bf[:, :], c_mur[:, _c:_c + 1], _y,
                    ALU.mult, ALU.add))
                dv_mix = b.op("vector", lambda v, _c=c, _y=yp: v.scalar_tensor_tensor(
                    xk_v[:, _c, :], d_bf[:, :], c_muk[:, _c:_c + 1], _y,
                    ALU.mult, ALU.add), inc=sDV)
            b.wait("tensor", sDV, dv_mix)

            def rhs_xr(kc, _t):
                return xr_v[:, kc, :]

            def rhs_xk(kc, _t):
                return xk_v[:, kc, :]

            def rhs_kact(kc, _t):
                return kact_v[:, kc, :]

            def evac_r(oc, _t, pp, te):
                b.wait("scalar", sTE, te)
                ac = b.op("scalar", lambda s, _oc=oc, _pp=pp: s.activation(
                    r_all[:, _oc, :], _pp[:, 0:512], AF.Sigmoid), inc=sAC)
                return (sAC, ac)

            kfree = [(sDV, 0), (sDV, 0)]

            def evac_k(oc, _t, pp, te):
                par = oc & 1
                b.wait("scalar", sTE, te)
                b.wait("scalar", *kfree[par])
                ac = b.op("scalar", lambda s, _p=par, _pp=pp: s.copy(
                    ksb_v[:, _p, :], _pp[:, 0:512]), inc=sAC)
                b.wait("vector", sAC, ac)
                dv = b.op("vector", lambda v, _oc=oc, _p=par: v.scalar_tensor_tensor(
                    kact_v[:, _oc, :], ksb_v[:, _p, :], 0.0, ksb_v[:, _p, :],
                    ALU.max, ALU.mult), inc=sDV)
                kfree[par] = (sDV, dv)
                return (sAC, ac)

            tvfree = [(sPL, 0), (sPL, 0)]

            def evac_vv(oc, _t, pp, te):
                par = oc & 1
                b.wait("vector", sTE, te)
                b.wait("vector", *tvfree[par])
                dv = b.op("vector", lambda v, _p=par, _oc=oc, _pp=pp: v.tensor_tensor(
                    tv_v[:, _p, :], _pp[:, 0:512], r_all[:, _oc, :], ALU.mult),
                    inc=sDV)
                b.wait("gpsimd", sDV, dv)
                pl = b.op("gpsimd", lambda g, _oc=oc, _p=par, _sl=sl: g.tensor_tensor(
                    R[:, _oc, _sl], R[:, _oc, _sl], tv_v[:, _p, :], ALU.add), inc=sPL)
                tvfree[par] = (sPL, pl)
                return (sDV, dv)

            def wsrc_w(wT):
                return lambda ocp, _w=wT: _w[:, ocp * 256: ocp * 256 + 256].rearrange(
                    "(c p) m -> p c m", p=P)

            wstream(wsrc_w(wrT), list(range(4)), [0], rhs_xr, evac_r)
            wstream(wsrc_w(wkT), list(range(4)), [0], rhs_xk, evac_k)
            b.wait("tensor", sDV)
            wstream(wsrc_w(wvT), list(range(4)), [0], rhs_kact, evac_vv)

    l0_ffn()
    if upto == 2:
        b.wait("sync", sPL)
        b.wait("sync", sDV)
        sdma(y, R)
        b.wait("sync", sSY)
        b.emit()
        return nc

    layer_norm(R_src, update_R=True, src_wait=(sPL, sDV))
    b.wait("sync", sPL)
    b.wait("sync", sDV)
    sp_r = b.op("sync", lambda sy: sy.dma_start(out=rsp1, in_=R), inc=sSP, by=16)

    def l1_attn():
        b.wait("tensor", sPL)
        def wsrc_pre(ocp):
            return wpre1[:, ocp * 256: ocp * 256 + 256].rearrange(
                "(c p) m -> p c m", p=P)

        def evac_pre(oc, tt, pp, te):
            b.wait("scalar", sTE, te)
            b.wait("scalar", sSY, sp_r)
            ac = b.op("scalar", lambda s, _oc=oc, _tt=tt, _pp=pp: s.activation(
                pre_v[:, _oc, 512 * _tt: 512 * _tt + 512], _pp[:, 0:512], AF.Silu,
                bias=c_bpre[:, _oc:_oc + 1], scale=1.0), inc=sAC)
            return (sAC, ac)

        wstream(wsrc_pre, list(range(4)), list(range(NT)), ysrc, evac_pre)
        for hp in range(NPAIR):
            def wsrc_kv(_e, _hp=hp):
                return wqkv1g[:, _hp, 128:384].rearrange("(c p) m -> p c m", p=P)

            def evac_kv(oc, tt, pp, te):
                b.wait("scalar", sTE, te)
                if oc % 2 == 0:
                    ac = b.op("scalar", lambda s, _tt=tt, _pp=pp, _hp=hp: s.copy(
                        K_all[:, _hp, 512 * _tt: 512 * _tt + 512], _pp[:, 0:512]),
                        inc=sAC)
                else:
                    ac = b.op("scalar", lambda s, _tt=tt, _pp=pp: s.copy(
                        VfB[:, 512 * _tt: 512 * _tt + 512], _pp[:, 0:512]), inc=sAC)
                return (sAC, ac)

            wstream(wsrc_kv, [0], list(range(NT)), ysrc, evac_kv)
            b.wait("tensor", sAC)
            transpose_to(VfB, VT_all[:, hp, :, :])
        b.wait("tensor", sAC)

        fe = fresh_fe()
        it_ctr = [0]
        pend = []
        lfree = [(sTE, 0), (sTE, 0)]
        qgfree = [(sTE, 0), (sTE, 0)]
        oq_done = [(sDV, 0), (sDV, 0)]
        rbfree = [(sDV, 0), (sDV, 0)]
        fillers = []

        def q_item(gn, hpp):
            gpar = gn & 1
            srcs = [wqkv1g[:, 2 * hpp + j, 0:128].rearrange(
                "(c p) m -> p c m", p=P) for j in range(2)]
            si = load_slot(srcs)
            use_slot(si)
            te = None
            for sub in (0, 1):
                oc = 2 * hpp + sub
                bz_pj.wait("tensor")
                for kc in range(C):
                    te = mm(pj[:, 0:512], slotA[si][:, kc, sub * 128: sub * 128 + 128],
                            Y[:, kc, 1 + 512 * gn: 1 + 512 * gn + 512],
                            kc == 0, kc == C - 1,
                            inc=(sTE if kc == C - 1 else None))
                b.wait("scalar", sTE, te)
                b.wait("scalar", *qgfree[gpar])
                ac = b.op("scalar", lambda s, _gp=gpar, _oc=oc: s.mul(
                    Qg_v[:, _gp, _oc, :], pj[:, 0:512], 0.125), inc=sAC)
                bz_pj.set(sAC, ac)
            slot_free[si] = [(sTE, te)]

        def p_item(qt_, ocp):
            opar = qt_ & 1
            qsl_ = slice(qt_ * 128, qt_ * 128 + 128)
            si = load_slot(wproj1[:, ocp * 256: ocp * 256 + 256].rearrange(
                "(c p) m -> p c m", p=P))
            use_slot(si)
            te = None
            for sub in (0, 1):
                oc = 2 * ocp + sub
                par = oc & 1
                b.wait("sync", sSP, sp_r)
                for (s_, n_) in [rbfree[par]]:
                    b.wait("sync", s_, n_)
                ld = b.op("sync", lambda sy, _p=par, _oc=oc, _q=qsl_:
                          sy.dma_start(out=rbuf[:, _p, :], in_=rsp1[:, _oc, _q]),
                          inc=sRB[par], by=16)
                bz_qp.wait("tensor")
                b.wait("tensor", *oq_done[opar])
                for kc in range(C):
                    te = mm(qp[:, 0:128], slotA[si][:, kc, sub * 128: sub * 128 + 128],
                            Oq_v[:, opar, kc, :], kc == 0, kc == C - 1,
                            inc=(sTE if kc == C - 1 else None))
                b.wait("vector", sTE, te)
                b.wait("vector", sRB[par], ld)
                dv = b.op("vector", lambda v, _p=par, _oc=oc:
                          v.scalar_tensor_tensor(rbuf[:, _p, :], qp[:, 0:128],
                                                 c_bproj[:, _oc:_oc + 1],
                                                 rbuf[:, _p, :], ALU.add, ALU.add),
                          inc=sDV)
                bz_qp.set(sDV, dv)
                b.wait("sync", sDV, dv)
                dest = y if upto == 3 else rsp2
                sdma(dest[:, oc, qsl_], rbuf[:, par, :])
                rbfree[par] = (sDV, dv)
            slot_free[si] = [(sTE, te)]

        for hpp in range(4):
            q_item(0, hpp)
        b.wait("tensor", sAC)
        for qt in range(NQ):
            g = qt // 4
            qi = qt % 4
            lg_par = qt & 1
            qsl = slice(qt * 128, qt * 128 + 128)
            bz_bigLo.wait("tensor")
            bz_bigHi.wait("tensor")
            bz_big.wait("tensor")
            te_lg = None
            for kt in range(NT):
                ksl = slice(kt * 512, kt * 512 + 512)
                for kc in range(C):
                    te_lg = mm(big[:, ksl], pre_v[:, kc, qsl], pre_v[:, kc, ksl],
                               kc == 0, kc == C - 1,
                               inc=(sTE if (kc == C - 1 and kt == NT - 1)
                                    else None))
            b.wait("scalar", sTE, te_lg)
            b.wait("scalar", *lfree[lg_par])
            ac_lg = b.op("scalar", lambda s, _p=lg_par: s.activation(
                lg_v[:, _p, :], big[:, 0:T], AF.Copy, bias=0.0,
                scale=pi32[:, 0:1]), inc=sAC)
            bz_big.set(sAC, ac_lg)
            bz_bigLo.set(sAC, ac_lg)
            bz_bigHi.set(sAC, ac_lg)
            b.wait("tensor", sAC, ac_lg)
            for u in range(16):
                hp, h = u // 2, u % 2
                if u == 2:
                    if qt > 0:
                        oq_done[(qt - 1) & 1] = (sDV, sDV.n)
                        for ocp in range(4):
                            fillers.append(lambda _q=qt - 1, _o=ocp: p_item(_q, _o))
                    if qi == 3 and g + 1 < NG:
                        for hpp in range(4):
                            fillers.append(lambda _g=g + 1, _h=hpp: q_item(_g, _h))
                it = it_ctr[0]
                it_ctr[0] += 1
                if len(pend) == 2:
                    attn_pv(*pend.pop(0))
                dvn = attn_score_softmax(
                    it, qt,
                    Q_ap=Qg_v[64 * h: 64 * h + 64, g & 1, hp,
                              qi * 128: qi * 128 + 128],
                    K_ap=K_all[64 * h: 64 * h + 64, hp, :],
                    logits_par=lg_par, causal=False, fe=fe)
                pend.append((it, qt, h, dvn,
                             (lambda kc, _hp=hp, _h=h:
                              VT_all[:, _hp, kc, 64 * _h: 64 * _h + 64]),
                             Oq_v[:, qt & 1, hp, :], False, fe))
                if fillers:
                    fillers.pop(0)()
            while fillers:
                fillers.pop(0)()
            lfree[lg_par] = (sTE, sTE.n)
            if qi == 3:
                qgfree[g & 1] = (sTE, sTE.n)
            if qt > 0:
                fe["Ofree"][(qt - 1) & 1] = (sTE, sTE.n)
        while pend:
            attn_pv(*pend.pop(0))
        oq_done[(NQ - 1) & 1] = (sDV, sDV.n)
        for ocp in range(4):
            p_item(NQ - 1, ocp)

    l1_attn()
    if upto == 16:
        b.wait("sync", sDV)
        sdma(dbg1[:, 0, 0:512], rw3_v[:, 0, 0:512])
        sdma(dbg1[:, 1, 0:512], rw3_v[:, 1, 0:512])
        sdma(dbg1[:, 2:3, 0:512], rsp1[:, 0:1, 0:512])
        b.wait("sync", sSY)
        b.emit()
        return nc
    if upto == 3:
        b.wait("sync", sSY)
        b.emit()
        return nc

    def c0_src_ready(tt):
        par = tt & 1
        if tt == 0:
            b.wait("sync", sSY)
        b.wait("sync", sDV, sDV.n)
        b.wait("sync", sAC, sAC.n)
        ld = b.op("sync", lambda sy, _p=par, _tt=tt: sy.dma_start(
            out=Rt512[:, _p, :, :],
            in_=rsp2[:, :, 512 * _tt: 512 * _tt + 512]), inc=sRT[par], by=16)
        b.wait("vector", sRT[par], ld)
        b.wait("scalar", sRT[par], ld)

    def c0_src(c, sl):
        par = (sl.start // 512) & 1
        return Rt512[:, par, c, :]

    layer_norm(c0_src, update_R=False, src_ready=c0_src_ready)

    def ffn_layer(src3, n_kc, n_oc, wdram, evac):
        nkcg = n_kc // 8
        pend = [(ocp, kcg) for ocp in range(n_oc // 2) for kcg in range(nkcg)]
        si = load_slot(wdram(*pend[0]))
        for idx, (ocp, kcg) in enumerate(pend):
            nsi = load_slot(wdram(*pend[idx + 1])) if idx + 1 < len(pend) else None
            use_slot(si)
            te = None
            for o2 in range(2):
                oc = ocp * 2 + o2
                if kcg == 0:
                    (bz_bigA if o2 == 0 else bz_bigB).wait("tensor")
                for nt in range(NTH):
                    ps = big[:, (o2 * NTH + nt) * 512: (o2 * NTH + nt) * 512 + 512]
                    for kc8 in range(8):
                        kc = kcg * 8 + kc8
                        last = (kcg == nkcg - 1 and kc8 == 7)
                        te = mm(ps, slotA[si][:, kc8, o2 * 128: o2 * 128 + 128],
                                src3(kc, nt), kcg == 0 and kc8 == 0, last,
                                inc=(sTE if (last and o2 == 1 and nt == NTH - 1)
                                     else None))
            if kcg == nkcg - 1:
                for o2 in range(2):
                    oc = ocp * 2 + o2
                    tiles = [big[:, (o2 * NTH + nt) * 512: (o2 * NTH + nt) * 512 + 512]
                             for nt in range(NTH)]
                    edge = evac(oc, tiles, te)
                    (bz_bigA if o2 == 0 else bz_bigB).edges = [edge]
            slot_free[si] = [(sTE, te)]
            si = nsi

    def l1_ffn():
        b.wait("tensor", sDV)
        for hf in range(NHALF):
            h0 = hf * HTOK

            def yh(kc, nt, _h0=h0):
                return Y[:, kc, 1 + _h0 + nt * 512: 1 + _h0 + nt * 512 + 512]

            safree = [(sDV, 0), (sDV, 0)]

            def evac_w1(oc, tiles, te_n):
                par = oc & 1
                b.wait("scalar", sTE, te_n)
                b.wait("scalar", *safree[par])
                ac = None
                for nt in range(NTH):
                    ac = b.op("scalar", lambda s, _oc=oc, _p=par, _nt=nt,
                              _ps=tiles[nt]: s.activation(
                                  sa_v[:, _p, _nt * 512: _nt * 512 + 512], _ps,
                                  AF.Silu, bias=0.0, scale=c_s1[:, _oc:_oc + 1]),
                              inc=sAC)
                return (sAC, ac)

            def evac_w2(oc, tiles, te_n):
                par = oc & 1
                b.wait("vector", sTE, te_n)
                b.wait("vector", sAC)
                dv = None
                for nt in range(NTH):
                    dv = b.op("vector", lambda v, _oc=oc, _p=par, _nt=nt,
                              _ps=tiles[nt]: v.scalar_tensor_tensor(
                                  h_v[:, _oc, _nt * 512: _nt * 512 + 512], _ps,
                                  c_s2[:, _oc:_oc + 1],
                                  sa_v[:, _p, _nt * 512: _nt * 512 + 512],
                                  ALU.mult, ALU.mult), inc=sDV)
                safree[par] = (sDV, dv)
                return (sDV, dv)

            for ocp in range(16):
                for (wd, evac) in ((w1T, evac_w1), (w2T, evac_w2)):
                    si = load_slot(wd[:, ocp * 256:(ocp + 1) * 256].rearrange(
                        "(k p) o -> p k o", p=P))
                    use_slot(si)
                    te = None
                    for o2 in range(2):
                        oc = ocp * 2 + o2
                        (bz_bigA if o2 == 0 else bz_bigB).wait("tensor")
                        for nt in range(NTH):
                            ps = big[:, (o2 * NTH + nt) * 512:
                                     (o2 * NTH + nt) * 512 + 512]
                            for kc in range(C):
                                te = mm(ps, slotA[si][:, kc, o2 * 128: o2 * 128 + 128],
                                        yh(kc, nt), kc == 0, kc == C - 1,
                                        inc=(sTE if (kc == C - 1 and o2 == 1
                                                     and nt == NTH - 1) else None))
                    for o2 in range(2):
                        oc = ocp * 2 + o2
                        tiles = [big[:, (o2 * NTH + nt) * 512:
                                     (o2 * NTH + nt) * 512 + 512]
                                 for nt in range(NTH)]
                        edge = evac(oc, tiles, te)
                        (bz_bigA if o2 == 0 else bz_bigB).edges = [edge]
                    slot_free[si] = [(sTE, te)]

            def hsrc(kc, nt):
                return h_v[:, kc, nt * 512: nt * 512 + 512]

            def evac_h1(oc, tiles, te_n):
                b.wait("scalar", sTE, te_n)
                ac = None
                for nt in range(NTH):
                    ac = b.op("scalar", lambda s, _oc=oc, _nt=nt, _ps=tiles[nt]:
                              s.activation(h1g_v[:, _oc, _nt * 512: _nt * 512 + 512],
                                           _ps, AF.Gelu, bias=0.0,
                                           scale=c_sh1[:, _oc:_oc + 1]), inc=sAC)
                return (sAC, ac)

            b.wait("tensor", sDV)
            ffn_layer(hsrc, 32, 32,
                      lambda ocp, kcg: h1T[kcg * 1024:(kcg + 1) * 1024,
                                           ocp * 256:(ocp + 1) * 256].rearrange(
                                               "(k p) o -> p k o", p=P), evac_h1)

            def h1src(kc, nt):
                return h1g_v[:, kc, nt * 512: nt * 512 + 512]

            def evac_h2(oc, tiles, te_n):
                b.wait("scalar", sTE, te_n)
                ac = None
                for nt in range(NTH):
                    ac = b.op("scalar", lambda s, _oc=oc, _nt=nt, _ps=tiles[nt]:
                              s.activation(hg2_v[:, _oc, _nt * 512: _nt * 512 + 512],
                                           _ps, AF.Gelu, bias=0.0,
                                           scale=c_sh2[:, _oc:_oc + 1]), inc=sAC)
                return (sAC, ac)

            b.wait("tensor", sAC)
            ffn_layer(h1src, 32, 32,
                      lambda ocp, kcg: h2T[kcg * 1024:(kcg + 1) * 1024,
                                           ocp * 256:(ocp + 1) * 256].rearrange(
                                               "(k p) o -> p k o", p=P), evac_h2)

            def h2src(kc, nt):
                return hg2_v[:, kc, nt * 512: nt * 512 + 512]

            rwfree = [(sSY, 0), (sSY, 0)]

            def evac_w3(oc, tiles, te_n, _h0=h0):
                par = oc & 1
                for (s_, n_) in [rwfree[par]]:
                    b.wait("sync", s_, n_)
                ld = b.op("sync", lambda sy, _p=par, _oc=oc:
                          sy.dma_start(out=rw3_v[:, _p, 0:HTOK],
                                       in_=rsp2[:, _oc, _h0: _h0 + HTOK]),
                          inc=sRW[par], by=16)
                b.wait("vector", sTE, te_n)
                b.wait("vector", sRW[par], ld)
                dv = None
                for nt in range(NTH):
                    dv = b.op("vector", lambda v, _oc=oc, _p=par, _nt=nt,
                              _ps=tiles[nt]: v.scalar_tensor_tensor(
                                  rw3_v[:, _p, _nt * 512: _nt * 512 + 512], _ps,
                                  c_s3[:, _oc:_oc + 1],
                                  rw3_v[:, _p, _nt * 512: _nt * 512 + 512],
                                  ALU.mult, ALU.add), inc=sDV)
                b.wait("sync", sDV, dv)
                sdma(y[:, oc, _h0: _h0 + HTOK], rw3_v[:, par, 0:HTOK])
                rwfree[par] = (sDV, dv)
                return (sDV, dv)

            b.wait("tensor", sAC)
            ffn_layer(h2src, 32, C,
                      lambda ocp, kcg: w3T[kcg * 1024:(kcg + 1) * 1024,
                                           ocp * 256:(ocp + 1) * 256].rearrange(
                                               "(k p) o -> p k o", p=P), evac_w3)

    l1_ffn()
    b.wait("sync", sSY)
    b.wait("gpsimd", sGP)
    b.emit()
    return nc



def _prep_shared(kw):
    f32 = np.float32
    w = kw
    wqkv = np.asarray(w["L0_attn_w_qkv"], f32)
    wq0, wk0, wv0 = wqkv[0:1024], wqkv[1024:2048], wqkv[2048:3072]
    g0 = np.stack([np.concatenate([wq0[128 * p:128 * p + 128],
                                   wk0[128 * p:128 * p + 128],
                                   wv0[128 * p:128 * p + 128]], axis=0)
                   for p in range(8)], axis=0)
    wqkv0g = np.ascontiguousarray(g0.transpose(2, 0, 1))

    Wq1 = np.asarray(w["L1_attn_Wq"], f32)
    Wk1 = np.asarray(w["L1_attn_Wk"], f32)
    Wv1 = np.asarray(w["L1_attn_Wv"], f32)
    g1 = np.stack([np.concatenate([Wq1[:, 128 * p:128 * p + 128],
                                   Wk1[:, 128 * p:128 * p + 128],
                                   Wv1[:, 128 * p:128 * p + 128]], axis=1)
                   for p in range(8)], axis=1)
    wqkv1g = np.ascontiguousarray(g1)

    def colmaj(v, nchunk):
        return np.ascontiguousarray(np.asarray(v, f32).reshape(nchunk, 128).T)

    return {
        "wqkv0g": wqkv0g,
        "wproj0T": np.ascontiguousarray(np.asarray(w["L0_attn_w_proj"], f32).T),
        "wrT": np.ascontiguousarray(np.asarray(w["L0_ffn_W_r"], f32).T),
        "wkT": np.ascontiguousarray(np.asarray(w["L0_ffn_W_k"], f32).T),
        "wvT": np.ascontiguousarray(np.asarray(w["L0_ffn_W_v"], f32).T),
        "mur": colmaj(w["L0_ffn_mu_r"], 8),
        "muk": colmaj(w["L0_ffn_mu_k"], 8),
        "wqkv1g": wqkv1g,
        "wpre1": np.ascontiguousarray(np.asarray(w["L1_attn_Wpre"], f32)),
        "wproj1": np.ascontiguousarray(np.asarray(w["L1_attn_Wproj"], f32)),
        "bpre": colmaj(w["L1_attn_bpre"], 8),
        "bproj": colmaj(w["L1_attn_bproj"], 8),
        "pi": np.asarray(w["L1_attn_pi"], f32).reshape(1, 1),
        "w1T": np.ascontiguousarray(np.asarray(w["L1_ffn_w1_w"], f32).T),
        "w2T": np.ascontiguousarray(np.asarray(w["L1_ffn_w2_w"], f32).T),
        "h1T": np.ascontiguousarray(np.asarray(w["L1_ffn_h1_w"], f32).T),
        "h2T": np.ascontiguousarray(np.asarray(w["L1_ffn_h2_w"], f32).T),
        "w3T": np.ascontiguousarray(np.asarray(w["L1_ffn_w3_w"], f32).T),
        "s1": colmaj(w["L1_ffn_w1_s"], 32),
        "s2": colmaj(w["L1_ffn_w2_s"], 32),
        "sh1": colmaj(w["L1_ffn_h1_s"], 32),
        "sh2": colmaj(w["L1_ffn_h2_s"], 32),
        "s3": colmaj(w["L1_ffn_w3_s"], 8),
        "ones32": np.ones((1, 128), f32),
        "epsv": np.full((1, 1), 1e-5, f32),
        "onesb": np.ones((128, 1), f32),
        "ident": np.eye(128, dtype=f32),
        "maskc": np.where(np.arange(128)[None, :] > np.arange(128)[:, None],
                          np.float32(-1e30), np.float32(0.0)).astype(f32),
    }


def kernel(**kw):
    T = 2048
    x = np.asarray(kw["L0_attn_x"], np.float32)
    ncores = x.shape[0]
    shared = _prep_shared(kw)
    nc = build_nc(T=T, upto=4)
    in_maps = []
    for bi in range(ncores):
        m = dict(shared)
        m["xt"] = np.ascontiguousarray(x[bi].T)
        in_maps.append(m)
    res = run_bass_kernel_spmd(nc, in_maps, core_ids=list(range(ncores)))
    out = np.empty((ncores, T, 1024), np.float32)
    for bi, r in enumerate(res.results):
        yv = np.asarray(r["y"])
        out[bi] = yv.transpose(1, 0, 2).reshape(1024, T).T
    return out


# version 11
# speedup vs baseline: 1.2421x; 1.2421x over previous
import numpy as np
import concourse.bass as bass
import concourse.mybir as mybir
from concourse.bass_utils import run_bass_kernel_spmd

F32 = mybir.dt.float32
BF16 = mybir.dt.bfloat16
AF = mybir.ActivationFunctionType
ALU = mybir.AluOpType

P = 128
D = 1024
C = 8
NPAIR = 8
ENGINES = ("tensor", "vector", "scalar", "gpsimd", "sync")


class Sem:
    __slots__ = ("h", "n")


class B:
    def __init__(self, nc):
        self.nc = nc
        self.q = {e: [] for e in ENGINES}
        self.hw = {e: {} for e in ENGINES}

    def sem(self, name):
        s = Sem()
        s.h = self.nc.alloc_semaphore(name)
        s.n = 0
        return s

    def wait(self, eng, sem, n=None):
        v = sem.n if n is None else n
        if v <= 0:
            return
        if v <= self.hw[eng].get(id(sem), 0):
            return
        self.hw[eng][id(sem)] = v
        self.q[eng].append(lambda e, _s=sem.h, _v=v: e.wait_ge(_s, _v))

    def op(self, eng, fn, inc=None, by=1):
        if inc is None:
            self.q[eng].append(fn)
            return None
        inc.n += by

        def g(e, _fn=fn, _s=inc.h, _b=by):
            _fn(e).then_inc(_s, _b)

        self.q[eng].append(g)
        return inc.n

    def emit(self):
        with self.nc.Block() as blk:
            @blk.tensor
            def _(t):
                for f in self.q["tensor"]:
                    f(t)

            @blk.vector
            def _(v):
                for f in self.q["vector"]:
                    f(v)

            @blk.scalar
            def _(s):
                for f in self.q["scalar"]:
                    f(s)

            @blk.gpsimd
            def _(g):
                for f in self.q["gpsimd"]:
                    f(g)

            @blk.sync
            def _(sy):
                for f in self.q["sync"]:
                    f(sy)


class Busy:
    def __init__(self, b):
        self.b = b
        self.edges = []

    def set(self, sem, n=None):
        self.edges = [(sem, sem.n if n is None else n)]

    def wait(self, eng):
        for (s, n) in self.edges:
            self.b.wait(eng, s, n)


def build_nc(T=2048, upto=4):
    NT = T // 512
    NQ = T // 128
    NG = T // 512
    HTOK = min(T, 1024)
    NHALF = T // HTOK
    NTH = HTOK // 512
    nc = bass.Bass("TRN2", target_bir_lowering=False, debug=False)

    def din(name, shape, dt=F32):
        return nc.dram_tensor(name, list(shape), dt, kind="ExternalInput").ap()

    xt = din("xt", (D, T))
    wqkv0g = din("wqkv0g", (D, NPAIR, 384))
    wproj0T = din("wproj0T", (D, D))
    wrT = din("wrT", (D, D))
    wkT = din("wkT", (D, D))
    wvT = din("wvT", (D, D))
    murD = din("mur", (P, C))
    mukD = din("muk", (P, C))
    wqkv1g = din("wqkv1g", (D, NPAIR, 384))
    wpre1 = din("wpre1", (D, D))
    wproj1 = din("wproj1", (D, D))
    bpreD = din("bpre", (P, C))
    bprojD = din("bproj", (P, C))
    piD = din("pi", (1, 1))
    w1T = din("w1T", (D, 4096))
    w2T = din("w2T", (D, 4096))
    h1T = din("h1T", (4096, 4096))
    h2T = din("h2T", (4096, 4096))
    w3T = din("w3T", (4096, D))
    s1D = din("s1", (P, 32))
    s2D = din("s2", (P, 32))
    sh1D = din("sh1", (P, 32))
    sh2D = din("sh2", (P, 32))
    s3D = din("s3", (P, C))
    ones32D = din("ones32", (1, P))
    epsvD = din("epsv", (1, 1))
    onesbD = din("onesb", (P, 1))
    identD = din("ident", (P, P))
    maskD = din("maskc", (P, P))

    y = nc.dram_tensor("y", [P, C, T], F32, kind="ExternalOutput").ap()
    dbg1 = nc.dram_tensor("dbg1", [P, C, T], F32, kind="ExternalOutput").ap()
    dbg2 = nc.dram_tensor("dbg2", [P, C, T], F32, kind="ExternalOutput").ap()
    rsp1 = nc.dram_tensor("rsp1", [P, C, T], F32, kind="Internal").ap()
    rsp2 = nc.dram_tensor("rsp2", [P, C, T], F32, kind="Internal").ap()

    c_ones32 = nc.alloc_sbuf_tensor("c_ones32", [1, P], F32)
    c_onesb = nc.alloc_sbuf_tensor("c_onesb", [P, 1], BF16)
    c_ident = nc.alloc_sbuf_tensor("c_ident", [P, P], BF16)
    c_mask = nc.alloc_sbuf_tensor("c_mask", [P, P], BF16)
    c_mur = nc.alloc_sbuf_tensor("c_mur", [P, C], F32)
    c_muk = nc.alloc_sbuf_tensor("c_muk", [P, C], F32)
    c_bpre = nc.alloc_sbuf_tensor("c_bpre", [P, C], F32)
    c_bproj = nc.alloc_sbuf_tensor("c_bproj", [P, C], F32)
    c_s1 = nc.alloc_sbuf_tensor("c_s1", [P, 32], F32)
    c_s2 = nc.alloc_sbuf_tensor("c_s2", [P, 32], F32)
    c_sh1 = nc.alloc_sbuf_tensor("c_sh1", [P, 32], F32)
    c_sh2 = nc.alloc_sbuf_tensor("c_sh2", [P, 32], F32)
    c_s3 = nc.alloc_sbuf_tensor("c_s3", [P, C], F32)
    c_pi = nc.alloc_sbuf_tensor("c_pi", [1, 1], F32)
    c_eps = nc.alloc_sbuf_tensor("c_eps", [1, 1], F32)
    pi32 = nc.alloc_sbuf_tensor("pi32", [P, 1], F32)
    statv = nc.alloc_sbuf_tensor("statv", [P, 16], F32)

    Yt = nc.alloc_sbuf_tensor("Yt", [P, C * (T + 1)], BF16)
    Y = Yt[:].rearrange("p (c t) -> p c t", c=C)

    ring_t = nc.alloc_sbuf_tensor("ring", [P, 6144], BF16)

    SA_BYTES = 32768
    arenaS_t = nc.alloc_sbuf_tensor("arenaS", [P, SA_BYTES // 4], F32)
    AD_BYTES = max(C * T * 4 + 6 * T + C * T * 2,
                   6 * C * T + 16384 + 4 * T + 8192,
                   128 * HTOK)
    arenaD_t = nc.alloc_sbuf_tensor("arenaD", [P, AD_BYTES // 4], F32)

    def view(arena_t, byte_off, shape, dt):
        elt = 2 if dt == BF16 else 4
        assert byte_off % elt == 0
        ncols = 1
        for s in shape[1:]:
            ncols *= s
        hh = arena_t.bitcast(dt) if dt == BF16 else arena_t
        ap = hh[:][:, byte_off // elt: byte_off // elt + ncols]
        if len(shape) == 3:
            ap = ap.rearrange("p (a b) -> p a b", a=shape[1])
        elif len(shape) == 4:
            ap = ap.rearrange("p (a b c) -> p a b c", a=shape[1], b=shape[2])
        return ap

    t1_v = view(arenaS_t, 0, [P, 512], F32)
    t2_v = view(arenaS_t, 2048, [P, 2, 512], F32)
    rb_v = view(arenaS_t, 6144, [P, 2, 512], BF16)
    rsq_v = view(arenaS_t, 8192, [P, 2, 512], BF16)
    st_mean = view(arenaS_t, 10240, [P, 512], F32)[0:1, :]
    st_tmp = view(arenaS_t, 12288, [P, 512], F32)[0:1, :]
    st_rstd = view(arenaS_t, 14336, [P, 512], F32)[0:1, :]
    d_bf = view(arenaS_t, 16384, [P, 512], BF16)
    S_sb = view(arenaS_t, 0, [P, 2, T], F32)
    Pb = view(arenaS_t, 8 * T, [P, 2, T], BF16)
    PT_sb = view(arenaS_t, 12 * T, [P, 2, T // 128, P], BF16)
    ksb_v = view(arenaS_t, 0, [P, 2, 512], F32)
    tv_v = view(arenaS_t, 4096, [P, 2, 512], F32)
    sa_v = view(arenaS_t, 17408, [P, 2, HTOK], BF16)
    rw3_v = view(arenaS_t, 17408 + 4 * HTOK, [P, 2, HTOK], F32)

    R = view(arenaD_t, 0, [P, C, T], F32)
    a0 = C * T * 4
    Qp = view(arenaD_t, a0 + 0, [P, T], BF16)
    Kp = view(arenaD_t, a0 + 2 * T, [P, T], BF16)
    Vf0 = view(arenaD_t, a0 + 4 * T, [P, T], BF16)
    VT0 = view(arenaD_t, a0 + 4 * T, [P, T // 128, P], BF16)
    O_all = view(arenaD_t, a0 + 6 * T, [P, C, T], BF16)
    xr_v = view(arenaD_t, a0 + 0, [P, C, 512], BF16)
    xk_v = view(arenaD_t, a0 + C * 1024, [P, C, 512], BF16)
    r_all = view(arenaD_t, a0 + C * 2048, [P, C, 512], F32)
    kact_v = view(arenaD_t, a0 + C * 4096, [P, C, 512], BF16)
    pre_v = view(arenaD_t, 0, [P, C, T], BF16)
    K_all = view(arenaD_t, 2 * C * T, [P, C, T], BF16)
    VT_all = view(arenaD_t, 4 * C * T, [P, C, T // 128, P], BF16)
    bB = 6 * C * T
    VfB = view(arenaD_t, bB, [P, T], BF16)
    Qg_v = view(arenaD_t, bB, [P, 2, C, 512], BF16)
    Oq_v = view(arenaD_t, bB + 16384, [P, 2, C, 128], BF16)
    lg_v = view(arenaD_t, bB + 20480, [P, 2, T], BF16)
    rbuf = view(arenaD_t, bB + 20480 + 4 * T, [P, 2, 128], F32)
    Rt512 = view(arenaD_t, 0, [P, 2, C, 512], F32)
    h_v = view(arenaD_t, 0, [P, 32, HTOK], BF16)
    h1g_v = view(arenaD_t, 64 * HTOK, [P, 32, HTOK], BF16)
    hg2_v = h_v

    big = nc.alloc_psum_tensor("big", [P, 2048], F32)[:]
    tp = nc.alloc_psum_tensor("tp", [P, 1024], BF16)[:]
    pv = nc.alloc_psum_tensor("pv", [P, 512], F32)[:]
    qp_h = nc.alloc_psum_tensor("qp", [P, 512], F32)
    qp = qp_h[:]
    qp_bf = qp_h.bitcast(BF16)[:]
    pj = nc.alloc_psum_tensor("pj", [P, 512], F32)[:]

    b = B(nc)
    sSY = b.sem("sSY")
    sGP = b.sem("sGP")
    sW = [b.sem(f"sW{i}") for i in range(3)]
    sRB = [b.sem("sRB0"), b.sem("sRB1")]
    sRW = [b.sem("sRW0"), b.sem("sRW1")]
    sRT = [b.sem("sRT0"), b.sem("sRT1")]
    sSP = b.sem("sSP")
    sTE = b.sem("sTE")
    sDV = b.sem("sDV")
    sAC = b.sem("sAC")
    sPL = b.sem("sPL")

    bz_big = Busy(b)
    bz_bigLo = Busy(b)
    bz_bigHi = Busy(b)
    bz_bigA = Busy(b)
    bz_bigB = Busy(b)
    bz_tp = Busy(b)
    bz_pv = Busy(b)
    bz_qp = Busy(b)
    bz_pj = Busy(b)

    def mm(out, lh, rh, first, last, inc=None):
        return b.op("tensor",
                    lambda t, _o=out, _l=lh, _r=rh, _f=first, _s=last:
                    t.matmul(_o, _l, _r, start=_f, stop=_s), inc=inc)

    def gdma(out, in_, wait_list=()):
        for (s, n) in wait_list:
            b.wait("gpsimd", s, n)
        return b.op("gpsimd", lambda g, _o=out, _i=in_: g.dma_start(out=_o, in_=_i),
                    inc=sGP, by=16)

    def sdma(out, in_, wait_list=()):
        for (s, n) in wait_list:
            b.wait("sync", s, n)
        return b.op("sync", lambda sy, _o=out, _i=in_: sy.dma_start(out=_o, in_=_i),
                    inc=sSY, by=16)

    NSLOT = 3
    slotA = [view(ring_t, i * 4096, [P, 8, 256], BF16) for i in range(NSLOT)]
    slot_free = [[(sTE, 0)] for _ in range(NSLOT)]
    slot_loaded = [0] * NSLOT
    slot_rr = [0]

    def load_slot(src_ap, width=256, dst4=False):
        i = slot_rr[0] % NSLOT
        slot_rr[0] += 1
        if isinstance(src_ap, list):
            n = None
            for j, sap in enumerate(src_ap):
                for (s_, n_) in (slot_free[i] if j == 0 else ()):
                    b.wait("gpsimd", s_, n_)
                n = b.op("gpsimd",
                         lambda g, _o=slotA[i][:, :, 128 * j: 128 * j + 128],
                         _i=sap: g.dma_start(out=_o, in_=_i), inc=sW[i], by=16)
            slot_loaded[i] = n
            return i
        dst = slotA[i] if width == 256 else slotA[i][:, :, 0:width]
        for (s_, n_) in slot_free[i]:
            b.wait("gpsimd", s_, n_)
        n = b.op("gpsimd", lambda g, _o=dst, _i=src_ap:
                 g.dma_start(out=_o, in_=_i), inc=sW[i], by=16)
        slot_loaded[i] = n
        return i

    def use_slot(i):
        b.wait("tensor", sW[i], slot_loaded[i])

    for dst, src in ((c_ones32, ones32D), (c_mur, murD), (c_muk, mukD),
                     (c_bpre, bpreD), (c_bproj, bprojD), (c_s1, s1D), (c_s2, s2D),
                     (c_sh1, sh1D), (c_sh2, sh2D), (c_s3, s3D), (c_pi, piD),
                     (c_eps, epsvD)):
        sdma(dst[:], src)
    n_const_sy = sSY.n
    gdma(c_onesb[:], onesbD)
    gdma(c_ident[:], identD)
    gdma(c_mask[:], maskD)
    n_const_gp = sGP.n
    ld_x = sdma(R, xt.rearrange("(c p) t -> p c t", p=P))
    b.wait("tensor", sSY, n_const_sy)
    mm(qp[:, 0:1], c_ones32[:], c_pi[:], True, True, inc=sTE)
    b.wait("vector", sTE)
    b.op("vector", lambda v: v.tensor_scalar(pi32[:], qp[:, 0:1], 1.0 / 32.0, None,
                                             ALU.mult), inc=sDV)
    bz_qp.set(sDV)
    b.op("vector", lambda v: v.memset(Y[:, :, 0:1], 0.0), inc=sDV)

    def layer_norm(src3, update_R=False, src_ready=None, src_wait=()):
        for s_ in src_wait:
            b.wait("vector", s_)
            b.wait("scalar", s_)
        t2free = [(sPL, 0), (sPL, 0)]
        for tt in range(NT):
            sl = slice(512 * tt, 512 * tt + 512)
            if src_ready is not None:
                src_ready(tt)
            bz_qp.wait("tensor")
            bz_pv.wait("tensor")
            te_c = [0, 0]
            for c in range(C):
                par = c & 1
                if te_c[par]:
                    b.wait("vector", sTE, te_c[par])
                    b.wait("scalar", sTE, te_c[par])
                dv = b.op("vector", lambda v, _c=c, _p=par, _sl=sl:
                          v.tensor_copy(rb_v[:, _p, :], src3(_c, _sl)), inc=sDV)
                ac = b.op("scalar", lambda s, _c=c, _p=par, _sl=sl:
                          s.square(rsq_v[:, _p, :], src3(_c, _sl)), inc=sAC)
                b.wait("tensor", sDV, dv)
                b.wait("tensor", sAC, ac)
                mm(qp[0:1, 0:512], c_onesb[:], rb_v[:, par, :], c == 0, c == C - 1)
                te_c[par] = mm(pv[0:1, 0:512], c_onesb[:], rsq_v[:, par, :],
                               c == 0, c == C - 1, inc=sTE)
            te_stats = sTE.n
            b.wait("vector", sTE, te_stats)
            dv_mean = b.op("vector", lambda v: v.tensor_scalar(
                st_mean, qp[0:1, 0:512], 1.0 / 1024.0, None, ALU.mult), inc=sDV)
            b.wait("scalar", sDV, dv_mean)
            ac_m2 = b.op("scalar", lambda s: s.square(st_tmp, st_mean), inc=sAC)
            b.wait("vector", sAC, ac_m2)
            dv_var = b.op("vector", lambda v: v.scalar_tensor_tensor(
                st_tmp, pv[0:1, 0:512], 1.0 / 1024.0, st_tmp,
                ALU.mult, ALU.subtract), inc=sDV)
            b.wait("scalar", sDV, dv_var)
            ac_sd = b.op("scalar", lambda s: s.activation(
                st_tmp, st_tmp, AF.Sqrt, bias=c_eps[:], scale=1.0), inc=sAC)
            b.wait("vector", sAC, ac_sd)
            dv_rstd = b.op("vector", lambda v: v.reciprocal(st_rstd, st_tmp), inc=sDV)
            bz_qp.set(sDV, dv_mean)
            bz_pv.set(sDV, dv_var)
            b.wait("tensor", sDV, dv_rstd)
            bz_big.wait("tensor")
            bz_bigLo.wait("tensor")
            bz_bigHi.wait("tensor")
            mm(big[:, 0:512], c_ones32[:], st_mean, True, True)
            te_bc = mm(big[:, 512:1024], c_ones32[:], st_rstd, True, True, inc=sTE)
            b.wait("vector", sTE, te_bc)
            last_dv = None
            for c in range(C):
                b.op("vector", lambda v, _c=c, _sl=sl: v.tensor_tensor(
                    t1_v, src3(_c, _sl), big[:, 0:512], ALU.subtract))
                if update_R:
                    par = c & 1
                    b.wait("vector", *t2free[par])
                    dvn = b.op("vector", lambda v, _p=par: v.tensor_tensor(
                        t2_v[:, _p, :], t1_v, big[:, 512:1024], ALU.mult), inc=sDV)
                    b.wait("gpsimd", sDV, dvn)
                    b.op("gpsimd", lambda g, _c=c, _p=par, _sl=sl: g.tensor_copy(
                        Y[:, _c, 1 + _sl.start: 1 + _sl.stop], t2_v[:, _p, :]))
                    pl = b.op("gpsimd", lambda g, _c=c, _p=par, _sl=sl: g.tensor_tensor(
                        R[:, _c, _sl], R[:, _c, _sl], t2_v[:, _p, :], ALU.add), inc=sPL)
                    t2free[par] = (sPL, pl)
                    last_dv = dvn
                else:
                    last_dv = b.op("vector", lambda v, _c=c, _sl=sl: v.tensor_tensor(
                        Y[:, _c, 1 + _sl.start: 1 + _sl.stop], t1_v,
                        big[:, 512:1024], ALU.mult), inc=sDV)
            if update_R:
                bz_big.set(sPL)
                bz_bigLo.set(sPL)
            else:
                bz_big.set(sDV, last_dv)
                bz_bigLo.set(sDV, last_dv)

    def R_src(c, sl):
        return R[:, c, sl]

    b.wait("vector", sSY, ld_x)
    b.wait("scalar", sSY, ld_x)
    b.wait("gpsimd", sGP, n_const_gp)
    if upto == 0:
        b.wait("sync", sSY, ld_x)
        sdma(y, R)
        b.wait("sync", sSY)
        b.emit()
        return nc
    layer_norm(R_src)
    if upto == 10:
        b.wait("gpsimd", sDV)
        b.wait("gpsimd", sPL)
        gdma(y, Y[:, :, 1:T + 1])
        b.wait("gpsimd", sGP)
        b.emit()
        return nc

    def wstream(wsrc, oc_list, tiles, rhs_fn, evac, N=512, wwidth=256, dst4=False):
        si = None
        nexts = list(oc_list)
        si = load_slot(wsrc(nexts[0]), wwidth, dst4)
        for i, oce in enumerate(nexts):
            nsi = (load_slot(wsrc(nexts[i + 1]), wwidth, dst4)
                   if i + 1 < len(nexts) else None)
            use_slot(si)
            te = None
            for sub in ([0, 128] if wwidth == 256 else [0]):
                oc = oce if wwidth == 128 else oce * 2 + (sub // 128)
                for ti in tiles:
                    ppsel = (qp, bz_qp) if (oc & 1) == 0 else (pj, bz_pj)
                    pp, bz = ppsel
                    bz.wait("tensor")
                    for kc in range(C):
                        te = mm(pp[:, 0:N], slotA[si][:, kc, sub: sub + 128],
                                rhs_fn(kc, ti), kc == 0, kc == C - 1,
                                inc=(sTE if kc == C - 1 else None))
                    edge = evac(oc, ti, pp, te)
                    bz.edges = [edge]
            slot_free[si] = [(sTE, te)]
            si = nsi

    def attn_score_softmax(it, qt, Q_ap, K_ap, logits_par, causal, fe):
        par = it & 1
        nk = qt + 1 if causal else NQ
        kl = nk * 128
        nkt = (kl + 511) // 512
        te_s = None
        te_lo = None
        for kt in range(nkt):
            if kt < 2:
                bz_bigLo.wait("tensor")
            else:
                bz_bigHi.wait("tensor")
            n = min(512, kl - kt * 512)
            ksl = slice(kt * 512, kt * 512 + n)
            diag_here = causal and (qt * 128 >= kt * 512) and (qt * 128 < kt * 512 + n)
            lg_here = logits_par is not None
            last_plain = not (diag_here or lg_here)
            lo_end = (kt == 1 or kt == nkt - 1) and kl > 1024 and kt < 2
            te_s = mm(big[:, ksl], Q_ap, K_ap[:, ksl], True, last_plain,
                      inc=(sTE if (last_plain and (kt == nkt - 1 or lo_end))
                           else None))
            if lg_here:
                te_s = mm(big[:, ksl], c_ident[:], lg_v[:, logits_par, ksl],
                          False, not diag_here,
                          inc=(sTE if ((not diag_here) and (kt == nkt - 1 or lo_end))
                               else None))
            if diag_here:
                dsl = slice(qt * 128, qt * 128 + 128)
                te_s = mm(big[:, dsl], c_ident[:], c_mask[:], False, True,
                          inc=(sTE if (kt == nkt - 1 or lo_end) else None))
            if lo_end:
                te_lo = te_s
        base = par * 8
        mxA = statv[:, base: base + 1]
        mxB = statv[:, base + 1: base + 2]
        nm = statv[:, base + 2: base + 3]
        zz = statv[:, base + 3: base + 4]
        rz = statv[:, base + 4: base + 5]
        b.wait("vector", *fe["Sfree"][par])
        if kl > 1024:
            b.wait("vector", sTE, te_lo)
            dv_lo = b.op("vector", lambda v, _p=par, _m=mxA: v.tensor_scalar(
                S_sb[:, _p, 0:1024], big[:, 0:1024], 0.0, None, ALU.add, ALU.max,
                accum_out=_m), inc=sDV)
            bz_bigLo.set(sDV, dv_lo)
            b.wait("vector", sTE, te_s)
            dv_evac = b.op("vector", lambda v, _p=par, _k=kl, _m=mxB: v.tensor_scalar(
                S_sb[:, _p, 1024:_k], big[:, 1024:_k], 0.0, None, ALU.add, ALU.max,
                accum_out=_m), inc=sDV)
            bz_bigHi.set(sDV, dv_evac)
            b.wait("vector", sDV, dv_evac)
            b.op("vector", lambda v, _a=mxA, _b2=mxB: v.tensor_max(_a, _a, _b2),
                 inc=sDV)
            b.wait("vector", sDV)
            dv_nm = b.op("vector", lambda v, _m=mxA, _n=nm: v.tensor_scalar(
                _n, _m, -1.0, None, ALU.mult), inc=sDV)
        else:
            b.wait("vector", sTE, te_s)
            dv_evac = b.op("vector", lambda v, _p=par, _k=kl, _m=mxA: v.tensor_scalar(
                S_sb[:, _p, 0:_k], big[:, 0:_k], 0.0, None, ALU.add, ALU.max,
                accum_out=_m), inc=sDV)
            bz_bigLo.set(sDV, dv_evac)
            bz_bigHi.set(sDV, dv_evac)
            b.wait("vector", sDV, dv_evac)
            dv_nm = b.op("vector", lambda v, _m=mxA, _n=nm: v.tensor_scalar(
                _n, _m, -1.0, None, ALU.mult), inc=sDV)
        b.wait("scalar", sDV, dv_nm)
        b.wait("scalar", *fe["Pfree"][par])
        ac_exp = b.op("scalar", lambda s, _p=par, _k=kl, _n=nm, _z=zz: s.activation(
            Pb[:, _p, 0:_k], S_sb[:, _p, 0:_k], AF.Exp, bias=_n, scale=1.0,
            accum_out=_z), inc=sAC)
        fe["Sfree"][par] = (sAC, ac_exp)
        return ac_exp

    def attn_pv(it, qt, h, ac_exp, VT_ap, O_dst, causal, fe):
        par = it & 1
        nk = qt + 1 if causal else NQ
        zz = statv[:, par * 8 + 3: par * 8 + 4]
        rz = statv[:, par * 8 + 4: par * 8 + 5]
        b.wait("vector", sAC, ac_exp)
        dv_rz = b.op("vector", lambda v, _z=zz, _r=rz: v.reciprocal(_r, _z), inc=sDV)
        b.wait("tensor", sAC, ac_exp)
        pt_edges = []
        for w in range((nk + 7) // 8):
            wn = min(8, nk - w * 8)
            wi = wave_ctr[0] & 1
            wave_ctr[0] += 1
            bank = tp if wi == 0 else qp_bf
            bz_bank = bz_tp if wi == 0 else bz_qp
            bz_bank.wait("tensor")
            te_w = None
            for j in range(wn):
                kc = w * 8 + j
                te_w = b.op("tensor", lambda t, _j=j, _kc=kc, _p=par, _b=bank:
                            t.transpose(_b[:, _j * 128: _j * 128 + 128],
                                        Pb[:, _p, _kc * 128: _kc * 128 + 128],
                                        c_ident[:]),
                            inc=(sTE if j == wn - 1 else None))
            eng = "vector" if wi == 0 else "scalar"
            esem = sDV if wi == 0 else sAC
            b.wait(eng, sTE, te_w)
            for (s_, n_) in [fe["PTfree"][par]]:
                b.wait(eng, s_, n_)
            if wi == 0:
                ac_pt = b.op(eng, lambda e, _w=w, _wn=wn, _p=par, _b=bank:
                             e.tensor_copy(PT_sb[:, _p, _w * 8: _w * 8 + _wn, :],
                                           _b[:, 0: _wn * 128]), inc=esem)
            else:
                ac_pt = b.op(eng, lambda e, _w=w, _wn=wn, _p=par, _b=bank:
                             e.copy(PT_sb[:, _p, _w * 8: _w * 8 + _wn, :],
                                    _b[:, 0: _wn * 128]), inc=esem)
            bz_bank.set(esem, ac_pt)
            pt_edges.append((esem, ac_pt))
        fe["Pfree"][par] = (sTE, sTE.n)
        for (s_, n_) in pt_edges:
            b.wait("tensor", s_, n_)
        if h == 0:
            bz_pv.wait("tensor")
        te_pv = None
        for kc in range(nk):
            te_pv = mm(pv[:, 64 * h: 64 * h + 64], PT_sb[:, par, kc, :], VT_ap(kc),
                       kc == 0, kc == nk - 1,
                       inc=(sTE if kc == nk - 1 else None))
        fe["PTfree"][par] = (sTE, te_pv)
        if h == 1:
            rz0 = statv[:, ((it - 1) & 1) * 8 + 4: ((it - 1) & 1) * 8 + 5]
            b.wait("scalar", sTE, te_pv)
            b.wait("scalar", sDV, dv_rz)
            for (s_, n_) in [fe["Ofree"][par]]:
                b.wait("scalar", s_, n_)
            b.op("scalar", lambda s, _d=O_dst, _r=rz0: s.mul(
                _d[:, 0:64], pv[:, 0:64], _r))
            ac_o = b.op("scalar", lambda s, _d=O_dst, _r=rz: s.mul(
                _d[:, 64:128], pv[:, 64:128], _r), inc=sAC)
            bz_pv.set(sAC, ac_o)
            wi = wave_ctr[0] & 1
            wave_ctr[0] += 1
            bank = tp if wi == 0 else qp_bf
            bz_bank = bz_tp if wi == 0 else bz_qp
            bz_bank.wait("tensor")
            b.wait("tensor", sAC, ac_o)
            te_ot = b.op("tensor", lambda t, _d=O_dst, _b=bank: t.transpose(
                _b[:, 0:128], _d[:, 0:128], c_ident[:]), inc=sTE)
            b.wait("vector", sTE, te_ot)
            dv_o = b.op("vector", lambda v, _d=O_dst, _b=bank: v.tensor_copy(
                _d[:, 0:128], _b[:, 0:128]), inc=sDV)
            bz_bank.set(sDV, dv_o)

    def fresh_fe():
        return {"Sfree": [(sDV, 0), (sDV, 0)],
                "Pfree": [(sDV, 0), (sDV, 0)],
                "PTfree": [(sDV, 0), (sDV, 0)],
                "Ofree": [(sDV, 0), (sDV, 0)]}

    def transpose_to(src_fm, dst_vt):
        for w in range((NQ + 7) // 8):
            wn = min(8, NQ - w * 8)
            bz_tp.wait("tensor")
            te_w = None
            for j in range(wn):
                tc = w * 8 + j
                te_w = b.op("tensor", lambda t, _j=j, _tc=tc: t.transpose(
                    tp[:, _j * 128: _j * 128 + 128],
                    src_fm[:, _tc * 128: _tc * 128 + 128], c_ident[:]),
                    inc=(sTE if j == wn - 1 else None))
            b.wait("scalar", sTE, te_w)
            ac_v = b.op("scalar", lambda s, _w=w, _wn=wn: s.copy(
                dst_vt[:, _w * 8: _w * 8 + _wn, :], tp[:, 0: _wn * 128]), inc=sAC)
            bz_tp.set(sAC, ac_v)

    def ysrc(kc, tt):
        return Y[:, kc, 1 + 512 * tt: 1 + 512 * tt + 512]

    wave_ctr = [0]

    def l0_attn(dump_o=False):
        fe = fresh_fe()
        it_ctr = [0]
        pend = []
        b.wait("tensor", sDV)
        for hp in range(NPAIR):
            def wsrc_qk(_e, _hp=hp):
                return wqkv0g[:, _hp, 0:256].rearrange("(c p) m -> p c m", p=P)

            def evac_qk(oc, tt, pp, te):
                b.wait("scalar", sTE, te)
                if oc % 2 == 0:
                    ac = b.op("scalar", lambda s, _tt=tt, _pp=pp: s.mul(
                        Qp[:, 512 * _tt: 512 * _tt + 512], _pp[:, 0:512], 0.125),
                        inc=sAC)
                else:
                    ac = b.op("scalar", lambda s, _tt=tt, _pp=pp: s.copy(
                        Kp[:, 512 * _tt: 512 * _tt + 512], _pp[:, 0:512]), inc=sAC)
                return (sAC, ac)

            wstream(wsrc_qk, [0], list(range(NT)), ysrc, evac_qk)

            def wsrc_v(_e, _hp=hp):
                return wqkv0g[:, _hp, 256:384].rearrange("(c p) m -> p c m", p=P)

            def evac_v(oc, tt, pp, te):
                b.wait("scalar", sTE, te)
                ac = b.op("scalar", lambda s, _tt=tt, _pp=pp: s.copy(
                    Vf0[:, 512 * _tt: 512 * _tt + 512], _pp[:, 0:512]), inc=sAC)
                return (sAC, ac)

            wstream(wsrc_v, [0], list(range(NT)), ysrc, evac_v, wwidth=128)
            b.wait("tensor", sAC)
            transpose_to(Vf0, VT0)
            b.wait("tensor", sAC)
            for qt in range(NQ):
                for h in (0, 1):
                    it = it_ctr[0]
                    it_ctr[0] += 1
                    if len(pend) == 2:
                        attn_pv(*pend.pop(0))
                    dvn = attn_score_softmax(
                        it, qt,
                        Q_ap=Qp[64 * h: 64 * h + 64, qt * 128: qt * 128 + 128],
                        K_ap=Kp[64 * h: 64 * h + 64, :],
                        logits_par=None, causal=True, fe=fe)
                    pend.append((it, qt, h, dvn,
                                 (lambda kc, _h=h: VT0[:, kc, 64 * _h: 64 * _h + 64]),
                                 O_all[:, hp, qt * 128: qt * 128 + 128], True, fe))
            while pend:
                attn_pv(*pend.pop(0))

        def wsrc_p0(ocp):
            return wproj0T[:, ocp * 256: ocp * 256 + 256].rearrange(
                "(c p) m -> p c m", p=P)

        def rhs_o(kc, tt):
            return O_all[:, kc, 512 * tt: 512 * tt + 512]

        def evac_p0(oc, tt, pp, te):
            b.wait("vector", sTE, te)
            dv = b.op("vector", lambda v, _oc=oc, _tt=tt, _pp=pp: v.tensor_tensor(
                R[:, _oc, 512 * _tt: 512 * _tt + 512], _pp[:, 0:512],
                R[:, _oc, 512 * _tt: 512 * _tt + 512], ALU.add), inc=sDV)
            return (sDV, dv)

        while pend:
            attn_pv(*pend.pop(0))
        if not dump_o:
            b.wait("tensor", sDV)
            wstream(wsrc_p0, list(range(4)), list(range(NT)), rhs_o, evac_p0)

    if upto == 11:
        fe = fresh_fe()
        b.wait("tensor", sDV)
        hp = 0
        def wsrc_qk(_e, _hp=hp):
            return wqkv0g[:, _hp, 0:256].rearrange("(c p) m -> p c m", p=P)
        def evac_qk(oc, tt, pp, te):
            b.wait("scalar", sTE, te)
            if oc % 2 == 0:
                ac = b.op("scalar", lambda s, _tt=tt, _pp=pp: s.mul(
                    Qp[:, 512 * _tt: 512 * _tt + 512], _pp[:, 0:512], 0.125),
                    inc=sAC)
            else:
                ac = b.op("scalar", lambda s, _tt=tt, _pp=pp: s.copy(
                    Kp[:, 512 * _tt: 512 * _tt + 512], _pp[:, 0:512]), inc=sAC)
            return (sAC, ac)
        wstream(wsrc_qk, [0], list(range(NT)), ysrc, evac_qk)
        def wsrc_v(_e, _hp=hp):
            return wqkv0g[:, _hp, 256:384].rearrange("(c p) m -> p c m", p=P)
        def evac_v(oc, tt, pp, te):
            b.wait("scalar", sTE, te)
            ac = b.op("scalar", lambda s, _tt=tt, _pp=pp: s.copy(
                Vf0[:, 512 * _tt: 512 * _tt + 512], _pp[:, 0:512]), inc=sAC)
            return (sAC, ac)
        wstream(wsrc_v, [0], list(range(NT)), ysrc, evac_v, wwidth=128)
        b.wait("tensor", sAC)
        transpose_to(Vf0, VT0)
        b.wait("gpsimd", sAC)
        gdma(y[:, 0, :], Qp[:, :])
        gdma(y[:, 1, :], Kp[:, :])
        gdma(y[:, 2, 0:512], VT0[:, :, :].rearrange("p a b -> p (a b)")[:, 0:512])
        b.wait("gpsimd", sGP)
        b.emit()
        return nc
    if upto == 13:
        fe = fresh_fe()
        b.wait("tensor", sDV)
        hp = 0
        def wsrc_qk(_e, _hp=hp):
            return wqkv0g[:, _hp, 0:256].rearrange("(c p) m -> p c m", p=P)
        def evac_qk(oc, tt, pp, te):
            b.wait("scalar", sTE, te)
            if oc % 2 == 0:
                ac = b.op("scalar", lambda s, _tt=tt, _pp=pp: s.mul(
                    Qp[:, 512 * _tt: 512 * _tt + 512], _pp[:, 0:512], 0.125),
                    inc=sAC)
            else:
                ac = b.op("scalar", lambda s, _tt=tt, _pp=pp: s.copy(
                    Kp[:, 512 * _tt: 512 * _tt + 512], _pp[:, 0:512]), inc=sAC)
            return (sAC, ac)
        wstream(wsrc_qk, [0], list(range(NT)), ysrc, evac_qk)
        def wsrc_v(_e, _hp=hp):
            return wqkv0g[:, _hp, 256:384].rearrange("(c p) m -> p c m", p=P)
        def evac_v(oc, tt, pp, te):
            b.wait("scalar", sTE, te)
            ac = b.op("scalar", lambda s, _tt=tt, _pp=pp: s.copy(
                Vf0[:, 512 * _tt: 512 * _tt + 512], _pp[:, 0:512]), inc=sAC)
            return (sAC, ac)
        wstream(wsrc_v, [0], list(range(NT)), ysrc, evac_v, wwidth=128)
        b.wait("tensor", sAC)
        transpose_to(Vf0, VT0)
        b.wait("tensor", sAC)
        _dvn = attn_score_softmax(1, 1, Q_ap=Qp[0:64, 128:256],
                                  K_ap=Kp[0:64, :], logits_par=None,
                                  causal=True, fe=fe)
        attn_pv(1, 1, 0, _dvn, (lambda kc: VT0[:, kc, 0:64]),
                O_all[:, 0, 128:256], True, fe)
        b.wait("vector", sTE)
        b.op("vector", lambda v: v.tensor_copy(O_all[:, 1, 0:128], pv[:, 0:128]),
             inc=sDV)
        b.wait("gpsimd", sDV)
        b.wait("gpsimd", sAC)
        gdma(y[:, 0, 0:256], S_sb[:, 1, 0:256])
        gdma(y[:, 1, 0:256], Pb[:, 1, 0:256])
        gdma(y[:, 2, 0:16], statv[:])
        gdma(y[:, 3, 0:256], PT_sb[:, 1, :, :].rearrange("p a b -> p (a b)")[:, 0:256])
        gdma(y[:, 4, 0:128], O_all[:, 1, 0:128])
        b.wait("gpsimd", sGP)
        b.emit()
        return nc
    l0_attn(dump_o=(upto == 12))
    if upto == 12:
        b.wait("gpsimd", sDV)
        gdma(y, O_all)
        b.wait("gpsimd", sGP)
        b.emit()
        return nc
    if upto == 1:
        b.wait("sync", sDV)
        sdma(y, R)
        b.wait("sync", sSY)
        b.emit()
        return nc

    layer_norm(R_src, src_wait=(sDV,))

    def l0_ffn():
        for tt in range(NT):
            sl = slice(512 * tt, 512 * tt + 512)
            dv_mix = None
            for c in range(C):
                yc = Y[:, c, 1 + sl.start: 1 + sl.stop]
                yp = Y[:, c, sl.start: sl.stop]
                b.op("vector", lambda v, _a=yc, _b2=yp: v.tensor_tensor(
                    d_bf[:, :], _a, _b2, ALU.subtract))
                b.op("vector", lambda v, _c=c, _y=yp: v.scalar_tensor_tensor(
                    xr_v[:, _c, :], d_bf[:, :], c_mur[:, _c:_c + 1], _y,
                    ALU.mult, ALU.add))
                dv_mix = b.op("vector", lambda v, _c=c, _y=yp: v.scalar_tensor_tensor(
                    xk_v[:, _c, :], d_bf[:, :], c_muk[:, _c:_c + 1], _y,
                    ALU.mult, ALU.add), inc=sDV)
            b.wait("tensor", sDV, dv_mix)

            def rhs_xr(kc, _t):
                return xr_v[:, kc, :]

            def rhs_xk(kc, _t):
                return xk_v[:, kc, :]

            def rhs_kact(kc, _t):
                return kact_v[:, kc, :]

            def evac_r(oc, _t, pp, te):
                b.wait("scalar", sTE, te)
                ac = b.op("scalar", lambda s, _oc=oc, _pp=pp: s.activation(
                    r_all[:, _oc, :], _pp[:, 0:512], AF.Sigmoid), inc=sAC)
                return (sAC, ac)

            kfree = [(sDV, 0), (sDV, 0)]

            def evac_k(oc, _t, pp, te):
                par = oc & 1
                b.wait("scalar", sTE, te)
                b.wait("scalar", *kfree[par])
                ac = b.op("scalar", lambda s, _p=par, _pp=pp: s.copy(
                    ksb_v[:, _p, :], _pp[:, 0:512]), inc=sAC)
                b.wait("vector", sAC, ac)
                dv = b.op("vector", lambda v, _oc=oc, _p=par: v.scalar_tensor_tensor(
                    kact_v[:, _oc, :], ksb_v[:, _p, :], 0.0, ksb_v[:, _p, :],
                    ALU.max, ALU.mult), inc=sDV)
                kfree[par] = (sDV, dv)
                return (sAC, ac)

            tvfree = [(sPL, 0), (sPL, 0)]

            def evac_vv(oc, _t, pp, te):
                par = oc & 1
                b.wait("vector", sTE, te)
                b.wait("vector", *tvfree[par])
                dv = b.op("vector", lambda v, _p=par, _oc=oc, _pp=pp: v.tensor_tensor(
                    tv_v[:, _p, :], _pp[:, 0:512], r_all[:, _oc, :], ALU.mult),
                    inc=sDV)
                b.wait("gpsimd", sDV, dv)
                pl = b.op("gpsimd", lambda g, _oc=oc, _p=par, _sl=sl: g.tensor_tensor(
                    R[:, _oc, _sl], R[:, _oc, _sl], tv_v[:, _p, :], ALU.add), inc=sPL)
                tvfree[par] = (sPL, pl)
                return (sDV, dv)

            def wsrc_w(wT):
                return lambda ocp, _w=wT: _w[:, ocp * 256: ocp * 256 + 256].rearrange(
                    "(c p) m -> p c m", p=P)

            wstream(wsrc_w(wrT), list(range(4)), [0], rhs_xr, evac_r)
            wstream(wsrc_w(wkT), list(range(4)), [0], rhs_xk, evac_k)
            b.wait("tensor", sDV)
            wstream(wsrc_w(wvT), list(range(4)), [0], rhs_kact, evac_vv)

    l0_ffn()
    if upto == 2:
        b.wait("sync", sPL)
        b.wait("sync", sDV)
        sdma(y, R)
        b.wait("sync", sSY)
        b.emit()
        return nc

    layer_norm(R_src, update_R=True, src_wait=(sPL, sDV))
    b.wait("sync", sPL)
    b.wait("sync", sDV)
    sp_r = b.op("sync", lambda sy: sy.dma_start(out=rsp1, in_=R), inc=sSP, by=16)

    def l1_attn():
        b.wait("tensor", sPL)
        def wsrc_pre(ocp):
            return wpre1[:, ocp * 256: ocp * 256 + 256].rearrange(
                "(c p) m -> p c m", p=P)

        def evac_pre(oc, tt, pp, te):
            b.wait("scalar", sTE, te)
            b.wait("scalar", sSY, sp_r)
            ac = b.op("scalar", lambda s, _oc=oc, _tt=tt, _pp=pp: s.activation(
                pre_v[:, _oc, 512 * _tt: 512 * _tt + 512], _pp[:, 0:512], AF.Silu,
                bias=c_bpre[:, _oc:_oc + 1], scale=1.0), inc=sAC)
            return (sAC, ac)

        wstream(wsrc_pre, list(range(4)), list(range(NT)), ysrc, evac_pre)
        for hp in range(NPAIR):
            def wsrc_kv(_e, _hp=hp):
                return wqkv1g[:, _hp, 128:384].rearrange("(c p) m -> p c m", p=P)

            def evac_kv(oc, tt, pp, te):
                b.wait("scalar", sTE, te)
                if oc % 2 == 0:
                    ac = b.op("scalar", lambda s, _tt=tt, _pp=pp, _hp=hp: s.copy(
                        K_all[:, _hp, 512 * _tt: 512 * _tt + 512], _pp[:, 0:512]),
                        inc=sAC)
                else:
                    ac = b.op("scalar", lambda s, _tt=tt, _pp=pp: s.copy(
                        VfB[:, 512 * _tt: 512 * _tt + 512], _pp[:, 0:512]), inc=sAC)
                return (sAC, ac)

            wstream(wsrc_kv, [0], list(range(NT)), ysrc, evac_kv)
            b.wait("tensor", sAC)
            transpose_to(VfB, VT_all[:, hp, :, :])
        b.wait("tensor", sAC)

        fe = fresh_fe()
        it_ctr = [0]
        pend = []
        lfree = [(sTE, 0), (sTE, 0)]
        qgfree = [(sTE, 0), (sTE, 0)]
        oq_done = [(sDV, 0), (sDV, 0)]
        rbfree = [(sDV, 0), (sDV, 0)]
        fillers = []

        def q_item(gn, hpp):
            gpar = gn & 1
            srcs = [wqkv1g[:, 2 * hpp + j, 0:128].rearrange(
                "(c p) m -> p c m", p=P) for j in range(2)]
            si = load_slot(srcs)
            use_slot(si)
            te = None
            for sub in (0, 1):
                oc = 2 * hpp + sub
                bz_pj.wait("tensor")
                for kc in range(C):
                    te = mm(pj[:, 0:512], slotA[si][:, kc, sub * 128: sub * 128 + 128],
                            Y[:, kc, 1 + 512 * gn: 1 + 512 * gn + 512],
                            kc == 0, kc == C - 1,
                            inc=(sTE if kc == C - 1 else None))
                b.wait("scalar", sTE, te)
                b.wait("scalar", *qgfree[gpar])
                ac = b.op("scalar", lambda s, _gp=gpar, _oc=oc: s.mul(
                    Qg_v[:, _gp, _oc, :], pj[:, 0:512], 0.125), inc=sAC)
                bz_pj.set(sAC, ac)
            slot_free[si] = [(sTE, te)]

        def p_item(qt_, ocp):
            opar = qt_ & 1
            qsl_ = slice(qt_ * 128, qt_ * 128 + 128)
            si = load_slot(wproj1[:, ocp * 256: ocp * 256 + 256].rearrange(
                "(c p) m -> p c m", p=P))
            use_slot(si)
            te = None
            for sub in (0, 1):
                oc = 2 * ocp + sub
                par = oc & 1
                b.wait("sync", sSP, sp_r)
                for (s_, n_) in [rbfree[par]]:
                    b.wait("sync", s_, n_)
                ld = b.op("sync", lambda sy, _p=par, _oc=oc, _q=qsl_:
                          sy.dma_start(out=rbuf[:, _p, :], in_=rsp1[:, _oc, _q]),
                          inc=sRB[par], by=16)
                bz_qp.wait("tensor")
                b.wait("tensor", *oq_done[opar])
                for kc in range(C):
                    te = mm(qp[:, 0:128], slotA[si][:, kc, sub * 128: sub * 128 + 128],
                            Oq_v[:, opar, kc, :], kc == 0, kc == C - 1,
                            inc=(sTE if kc == C - 1 else None))
                b.wait("vector", sTE, te)
                b.wait("vector", sRB[par], ld)
                dv = b.op("vector", lambda v, _p=par, _oc=oc:
                          v.scalar_tensor_tensor(rbuf[:, _p, :], qp[:, 0:128],
                                                 c_bproj[:, _oc:_oc + 1],
                                                 rbuf[:, _p, :], ALU.add, ALU.add),
                          inc=sDV)
                bz_qp.set(sDV, dv)
                b.wait("sync", sDV, dv)
                dest = y if upto == 3 else rsp2
                sdma(dest[:, oc, qsl_], rbuf[:, par, :])
                rbfree[par] = (sDV, dv)
            slot_free[si] = [(sTE, te)]

        for hpp in range(4):
            q_item(0, hpp)
        b.wait("tensor", sAC)
        for qt in range(NQ):
            g = qt // 4
            qi = qt % 4
            lg_par = qt & 1
            qsl = slice(qt * 128, qt * 128 + 128)
            bz_bigLo.wait("tensor")
            bz_bigHi.wait("tensor")
            bz_big.wait("tensor")
            te_lg = None
            for kt in range(NT):
                ksl = slice(kt * 512, kt * 512 + 512)
                for kc in range(C):
                    te_lg = mm(big[:, ksl], pre_v[:, kc, qsl], pre_v[:, kc, ksl],
                               kc == 0, kc == C - 1,
                               inc=(sTE if (kc == C - 1 and kt == NT - 1)
                                    else None))
            b.wait("scalar", sTE, te_lg)
            b.wait("scalar", *lfree[lg_par])
            ac_lg = b.op("scalar", lambda s, _p=lg_par: s.activation(
                lg_v[:, _p, :], big[:, 0:T], AF.Copy, bias=0.0,
                scale=pi32[:, 0:1]), inc=sAC)
            bz_big.set(sAC, ac_lg)
            bz_bigLo.set(sAC, ac_lg)
            bz_bigHi.set(sAC, ac_lg)
            b.wait("tensor", sAC, ac_lg)
            for u in range(16):
                hp, h = u // 2, u % 2
                if u == 2:
                    if qt > 0:
                        oq_done[(qt - 1) & 1] = (sDV, sDV.n)
                        for ocp in range(4):
                            fillers.append(lambda _q=qt - 1, _o=ocp: p_item(_q, _o))
                    if qi == 3 and g + 1 < NG:
                        for hpp in range(4):
                            fillers.append(lambda _g=g + 1, _h=hpp: q_item(_g, _h))
                it = it_ctr[0]
                it_ctr[0] += 1
                if len(pend) == 2:
                    attn_pv(*pend.pop(0))
                dvn = attn_score_softmax(
                    it, qt,
                    Q_ap=Qg_v[64 * h: 64 * h + 64, g & 1, hp,
                              qi * 128: qi * 128 + 128],
                    K_ap=K_all[64 * h: 64 * h + 64, hp, :],
                    logits_par=lg_par, causal=False, fe=fe)
                pend.append((it, qt, h, dvn,
                             (lambda kc, _hp=hp, _h=h:
                              VT_all[:, _hp, kc, 64 * _h: 64 * _h + 64]),
                             Oq_v[:, qt & 1, hp, :], False, fe))
                if fillers:
                    fillers.pop(0)()
            while fillers:
                fillers.pop(0)()
            lfree[lg_par] = (sTE, sTE.n)
            if qi == 3:
                qgfree[g & 1] = (sTE, sTE.n)
            if qt > 0:
                fe["Ofree"][(qt - 1) & 1] = (sTE, sTE.n)
        while pend:
            attn_pv(*pend.pop(0))
        oq_done[(NQ - 1) & 1] = (sDV, sDV.n)
        for ocp in range(4):
            p_item(NQ - 1, ocp)

    l1_attn()
    if upto == 16:
        b.wait("sync", sDV)
        sdma(dbg1[:, 0, 0:512], rw3_v[:, 0, 0:512])
        sdma(dbg1[:, 1, 0:512], rw3_v[:, 1, 0:512])
        sdma(dbg1[:, 2:3, 0:512], rsp1[:, 0:1, 0:512])
        b.wait("sync", sSY)
        b.emit()
        return nc
    if upto == 3:
        b.wait("sync", sSY)
        b.emit()
        return nc

    def c0_src_ready(tt):
        par = tt & 1
        if tt == 0:
            b.wait("sync", sSY)
        b.wait("sync", sDV, sDV.n)
        b.wait("sync", sAC, sAC.n)
        ld = b.op("sync", lambda sy, _p=par, _tt=tt: sy.dma_start(
            out=Rt512[:, _p, :, :],
            in_=rsp2[:, :, 512 * _tt: 512 * _tt + 512]), inc=sRT[par], by=16)
        b.wait("vector", sRT[par], ld)
        b.wait("scalar", sRT[par], ld)

    def c0_src(c, sl):
        par = (sl.start // 512) & 1
        return Rt512[:, par, c, :]

    layer_norm(c0_src, update_R=False, src_ready=c0_src_ready)

    def ffn_layer(src3, n_kc, n_oc, wdram, evac):
        nkcg = n_kc // 8
        pend = [(ocp, kcg) for ocp in range(n_oc // 2) for kcg in range(nkcg)]
        si = load_slot(wdram(*pend[0]))
        for idx, (ocp, kcg) in enumerate(pend):
            nsi = load_slot(wdram(*pend[idx + 1])) if idx + 1 < len(pend) else None
            use_slot(si)
            te = None
            for o2 in range(2):
                oc = ocp * 2 + o2
                if kcg == 0:
                    (bz_bigA if o2 == 0 else bz_bigB).wait("tensor")
                for nt in range(NTH):
                    ps = big[:, (o2 * NTH + nt) * 512: (o2 * NTH + nt) * 512 + 512]
                    for kc8 in range(8):
                        kc = kcg * 8 + kc8
                        last = (kcg == nkcg - 1 and kc8 == 7)
                        te = mm(ps, slotA[si][:, kc8, o2 * 128: o2 * 128 + 128],
                                src3(kc, nt), kcg == 0 and kc8 == 0, last,
                                inc=(sTE if (last and o2 == 1 and nt == NTH - 1)
                                     else None))
            if kcg == nkcg - 1:
                for o2 in range(2):
                    oc = ocp * 2 + o2
                    tiles = [big[:, (o2 * NTH + nt) * 512: (o2 * NTH + nt) * 512 + 512]
                             for nt in range(NTH)]
                    edge = evac(oc, tiles, te)
                    (bz_bigA if o2 == 0 else bz_bigB).edges = [edge]
            slot_free[si] = [(sTE, te)]
            si = nsi

    def l1_ffn():
        b.wait("tensor", sDV)
        for hf in range(NHALF):
            h0 = hf * HTOK

            def yh(kc, nt, _h0=h0):
                return Y[:, kc, 1 + _h0 + nt * 512: 1 + _h0 + nt * 512 + 512]

            safree = [(sDV, 0), (sDV, 0)]

            def evac_w1(oc, tiles, te_n):
                par = oc & 1
                b.wait("scalar", sTE, te_n)
                b.wait("scalar", *safree[par])
                ac = None
                for nt in range(NTH):
                    ac = b.op("scalar", lambda s, _oc=oc, _p=par, _nt=nt,
                              _ps=tiles[nt]: s.activation(
                                  sa_v[:, _p, _nt * 512: _nt * 512 + 512], _ps,
                                  AF.Silu, bias=0.0, scale=c_s1[:, _oc:_oc + 1]),
                              inc=sAC)
                return (sAC, ac)

            def evac_w2(oc, tiles, te_n):
                par = oc & 1
                b.wait("vector", sTE, te_n)
                b.wait("vector", sAC)
                dv = None
                for nt in range(NTH):
                    dv = b.op("vector", lambda v, _oc=oc, _p=par, _nt=nt,
                              _ps=tiles[nt]: v.scalar_tensor_tensor(
                                  h_v[:, _oc, _nt * 512: _nt * 512 + 512], _ps,
                                  c_s2[:, _oc:_oc + 1],
                                  sa_v[:, _p, _nt * 512: _nt * 512 + 512],
                                  ALU.mult, ALU.mult), inc=sDV)
                safree[par] = (sDV, dv)
                return (sDV, dv)

            for ocp in range(16):
                for (wd, evac) in ((w1T, evac_w1), (w2T, evac_w2)):
                    si = load_slot(wd[:, ocp * 256:(ocp + 1) * 256].rearrange(
                        "(k p) o -> p k o", p=P))
                    use_slot(si)
                    te = None
                    for o2 in range(2):
                        oc = ocp * 2 + o2
                        (bz_bigA if o2 == 0 else bz_bigB).wait("tensor")
                        for nt in range(NTH):
                            ps = big[:, (o2 * NTH + nt) * 512:
                                     (o2 * NTH + nt) * 512 + 512]
                            for kc in range(C):
                                te = mm(ps, slotA[si][:, kc, o2 * 128: o2 * 128 + 128],
                                        yh(kc, nt), kc == 0, kc == C - 1,
                                        inc=(sTE if (kc == C - 1 and o2 == 1
                                                     and nt == NTH - 1) else None))
                    for o2 in range(2):
                        oc = ocp * 2 + o2
                        tiles = [big[:, (o2 * NTH + nt) * 512:
                                     (o2 * NTH + nt) * 512 + 512]
                                 for nt in range(NTH)]
                        edge = evac(oc, tiles, te)
                        (bz_bigA if o2 == 0 else bz_bigB).edges = [edge]
                    slot_free[si] = [(sTE, te)]

            def hsrc(kc, nt):
                return h_v[:, kc, nt * 512: nt * 512 + 512]

            def evac_h1(oc, tiles, te_n):
                b.wait("scalar", sTE, te_n)
                ac = None
                for nt in range(NTH):
                    ac = b.op("scalar", lambda s, _oc=oc, _nt=nt, _ps=tiles[nt]:
                              s.activation(h1g_v[:, _oc, _nt * 512: _nt * 512 + 512],
                                           _ps, AF.Gelu, bias=0.0,
                                           scale=c_sh1[:, _oc:_oc + 1]), inc=sAC)
                return (sAC, ac)

            b.wait("tensor", sDV)
            ffn_layer(hsrc, 32, 32,
                      lambda ocp, kcg: h1T[kcg * 1024:(kcg + 1) * 1024,
                                           ocp * 256:(ocp + 1) * 256].rearrange(
                                               "(k p) o -> p k o", p=P), evac_h1)

            def h1src(kc, nt):
                return h1g_v[:, kc, nt * 512: nt * 512 + 512]

            def evac_h2(oc, tiles, te_n):
                b.wait("scalar", sTE, te_n)
                ac = None
                for nt in range(NTH):
                    ac = b.op("scalar", lambda s, _oc=oc, _nt=nt, _ps=tiles[nt]:
                              s.activation(hg2_v[:, _oc, _nt * 512: _nt * 512 + 512],
                                           _ps, AF.Gelu, bias=0.0,
                                           scale=c_sh2[:, _oc:_oc + 1]), inc=sAC)
                return (sAC, ac)

            b.wait("tensor", sAC)
            ffn_layer(h1src, 32, 32,
                      lambda ocp, kcg: h2T[kcg * 1024:(kcg + 1) * 1024,
                                           ocp * 256:(ocp + 1) * 256].rearrange(
                                               "(k p) o -> p k o", p=P), evac_h2)

            def h2src(kc, nt):
                return hg2_v[:, kc, nt * 512: nt * 512 + 512]

            rwfree = [(sSY, 0), (sSY, 0)]

            def evac_w3(oc, tiles, te_n, _h0=h0):
                par = oc & 1
                for (s_, n_) in [rwfree[par]]:
                    b.wait("sync", s_, n_)
                ld = b.op("sync", lambda sy, _p=par, _oc=oc:
                          sy.dma_start(out=rw3_v[:, _p, 0:HTOK],
                                       in_=rsp2[:, _oc, _h0: _h0 + HTOK]),
                          inc=sRW[par], by=16)
                b.wait("vector", sTE, te_n)
                b.wait("vector", sRW[par], ld)
                dv = None
                for nt in range(NTH):
                    dv = b.op("vector", lambda v, _oc=oc, _p=par, _nt=nt,
                              _ps=tiles[nt]: v.scalar_tensor_tensor(
                                  rw3_v[:, _p, _nt * 512: _nt * 512 + 512], _ps,
                                  c_s3[:, _oc:_oc + 1],
                                  rw3_v[:, _p, _nt * 512: _nt * 512 + 512],
                                  ALU.mult, ALU.add), inc=sDV)
                b.wait("sync", sDV, dv)
                sdma(y[:, oc, _h0: _h0 + HTOK], rw3_v[:, par, 0:HTOK])
                rwfree[par] = (sDV, dv)
                return (sDV, dv)

            b.wait("tensor", sAC)
            ffn_layer(h2src, 32, C,
                      lambda ocp, kcg: w3T[kcg * 1024:(kcg + 1) * 1024,
                                           ocp * 256:(ocp + 1) * 256].rearrange(
                                               "(k p) o -> p k o", p=P), evac_w3)

    l1_ffn()
    b.wait("sync", sSY)
    b.wait("gpsimd", sGP)
    b.emit()
    return nc



def _prep_shared(kw):
    f32 = np.float32
    w = kw
    wqkv = np.asarray(w["L0_attn_w_qkv"], f32)
    wq0, wk0, wv0 = wqkv[0:1024], wqkv[1024:2048], wqkv[2048:3072]
    g0 = np.stack([np.concatenate([wq0[128 * p:128 * p + 128],
                                   wk0[128 * p:128 * p + 128],
                                   wv0[128 * p:128 * p + 128]], axis=0)
                   for p in range(8)], axis=0)
    wqkv0g = np.ascontiguousarray(g0.transpose(2, 0, 1))

    Wq1 = np.asarray(w["L1_attn_Wq"], f32)
    Wk1 = np.asarray(w["L1_attn_Wk"], f32)
    Wv1 = np.asarray(w["L1_attn_Wv"], f32)
    g1 = np.stack([np.concatenate([Wq1[:, 128 * p:128 * p + 128],
                                   Wk1[:, 128 * p:128 * p + 128],
                                   Wv1[:, 128 * p:128 * p + 128]], axis=1)
                   for p in range(8)], axis=1)
    wqkv1g = np.ascontiguousarray(g1)

    def colmaj(v, nchunk):
        return np.ascontiguousarray(np.asarray(v, f32).reshape(nchunk, 128).T)

    return {
        "wqkv0g": wqkv0g,
        "wproj0T": np.ascontiguousarray(np.asarray(w["L0_attn_w_proj"], f32).T),
        "wrT": np.ascontiguousarray(np.asarray(w["L0_ffn_W_r"], f32).T),
        "wkT": np.ascontiguousarray(np.asarray(w["L0_ffn_W_k"], f32).T),
        "wvT": np.ascontiguousarray(np.asarray(w["L0_ffn_W_v"], f32).T),
        "mur": colmaj(w["L0_ffn_mu_r"], 8),
        "muk": colmaj(w["L0_ffn_mu_k"], 8),
        "wqkv1g": wqkv1g,
        "wpre1": np.ascontiguousarray(np.asarray(w["L1_attn_Wpre"], f32)),
        "wproj1": np.ascontiguousarray(np.asarray(w["L1_attn_Wproj"], f32)),
        "bpre": colmaj(w["L1_attn_bpre"], 8),
        "bproj": colmaj(w["L1_attn_bproj"], 8),
        "pi": np.asarray(w["L1_attn_pi"], f32).reshape(1, 1),
        "w1T": np.ascontiguousarray(np.asarray(w["L1_ffn_w1_w"], f32).T),
        "w2T": np.ascontiguousarray(np.asarray(w["L1_ffn_w2_w"], f32).T),
        "h1T": np.ascontiguousarray(np.asarray(w["L1_ffn_h1_w"], f32).T),
        "h2T": np.ascontiguousarray(np.asarray(w["L1_ffn_h2_w"], f32).T),
        "w3T": np.ascontiguousarray(np.asarray(w["L1_ffn_w3_w"], f32).T),
        "s1": colmaj(w["L1_ffn_w1_s"], 32),
        "s2": colmaj(w["L1_ffn_w2_s"], 32),
        "sh1": colmaj(w["L1_ffn_h1_s"], 32),
        "sh2": colmaj(w["L1_ffn_h2_s"], 32),
        "s3": colmaj(w["L1_ffn_w3_s"], 8),
        "ones32": np.ones((1, 128), f32),
        "epsv": np.full((1, 1), 1e-5, f32),
        "onesb": np.ones((128, 1), f32),
        "ident": np.eye(128, dtype=f32),
        "maskc": np.where(np.arange(128)[None, :] > np.arange(128)[:, None],
                          np.float32(-1e30), np.float32(0.0)).astype(f32),
    }


def kernel(**kw):
    T = 2048
    x = np.asarray(kw["L0_attn_x"], np.float32)
    ncores = x.shape[0]
    shared = _prep_shared(kw)
    nc = build_nc(T=T, upto=4)
    in_maps = []
    for bi in range(ncores):
        m = dict(shared)
        m["xt"] = np.ascontiguousarray(x[bi].T)
        in_maps.append(m)
    res = run_bass_kernel_spmd(nc, in_maps, core_ids=list(range(ncores)))
    out = np.empty((ncores, T, 1024), np.float32)
    for bi, r in enumerate(res.results):
        yv = np.asarray(r["y"])
        out[bi] = yv.transpose(1, 0, 2).reshape(1024, T).T
    return out


# version 12
# speedup vs baseline: 1.2453x; 1.0025x over previous
import numpy as np
import concourse.bass as bass
import concourse.mybir as mybir
from concourse.bass_utils import run_bass_kernel_spmd

F32 = mybir.dt.float32
BF16 = mybir.dt.bfloat16
AF = mybir.ActivationFunctionType
ALU = mybir.AluOpType

P = 128
D = 1024
C = 8
NPAIR = 8
ENGINES = ("tensor", "vector", "scalar", "gpsimd", "sync")


class Sem:
    __slots__ = ("h", "n")


class B:
    def __init__(self, nc):
        self.nc = nc
        self.q = {e: [] for e in ENGINES}
        self.hw = {e: {} for e in ENGINES}

    def sem(self, name):
        s = Sem()
        s.h = self.nc.alloc_semaphore(name)
        s.n = 0
        return s

    def wait(self, eng, sem, n=None):
        v = sem.n if n is None else n
        if v <= 0:
            return
        if v <= self.hw[eng].get(id(sem), 0):
            return
        self.hw[eng][id(sem)] = v
        self.q[eng].append(lambda e, _s=sem.h, _v=v: e.wait_ge(_s, _v))

    def op(self, eng, fn, inc=None, by=1):
        if inc is None:
            self.q[eng].append(fn)
            return None
        inc.n += by

        def g(e, _fn=fn, _s=inc.h, _b=by):
            _fn(e).then_inc(_s, _b)

        self.q[eng].append(g)
        return inc.n

    def emit(self):
        with self.nc.Block() as blk:
            @blk.tensor
            def _(t):
                for f in self.q["tensor"]:
                    f(t)

            @blk.vector
            def _(v):
                for f in self.q["vector"]:
                    f(v)

            @blk.scalar
            def _(s):
                for f in self.q["scalar"]:
                    f(s)

            @blk.gpsimd
            def _(g):
                for f in self.q["gpsimd"]:
                    f(g)

            @blk.sync
            def _(sy):
                for f in self.q["sync"]:
                    f(sy)


class Busy:
    def __init__(self, b):
        self.b = b
        self.edges = []

    def set(self, sem, n=None):
        self.edges = [(sem, sem.n if n is None else n)]

    def wait(self, eng):
        for (s, n) in self.edges:
            self.b.wait(eng, s, n)


def build_nc(T=2048, upto=4):
    NT = T // 512
    NQ = T // 128
    NG = T // 512
    HTOK = min(T, 1024)
    NHALF = T // HTOK
    NTH = HTOK // 512
    nc = bass.Bass("TRN2", target_bir_lowering=False, debug=False)

    def din(name, shape, dt=F32):
        return nc.dram_tensor(name, list(shape), dt, kind="ExternalInput").ap()

    xt = din("xt", (D, T))
    wqkv0g = din("wqkv0g", (D, NPAIR, 384))
    wproj0T = din("wproj0T", (D, D))
    wrT = din("wrT", (D, D))
    wkT = din("wkT", (D, D))
    wvT = din("wvT", (D, D))
    murD = din("mur", (P, C))
    mukD = din("muk", (P, C))
    wqkv1g = din("wqkv1g", (D, NPAIR, 384))
    wpre1 = din("wpre1", (D, D))
    wproj1 = din("wproj1", (D, D))
    bpreD = din("bpre", (P, C))
    bprojD = din("bproj", (P, C))
    piD = din("pi", (1, 1))
    w1T = din("w1T", (D, 4096))
    w2T = din("w2T", (D, 4096))
    h1T = din("h1T", (4096, 4096))
    h2T = din("h2T", (4096, 4096))
    w3T = din("w3T", (4096, D))
    s1D = din("s1", (P, 32))
    s2D = din("s2", (P, 32))
    sh1D = din("sh1", (P, 32))
    sh2D = din("sh2", (P, 32))
    s3D = din("s3", (P, C))
    ones32D = din("ones32", (1, P))
    epsvD = din("epsv", (1, 1))
    onesbD = din("onesb", (P, 1))
    identD = din("ident", (P, P))
    maskD = din("maskc", (P, P))

    y = nc.dram_tensor("y", [P, C, T], F32, kind="ExternalOutput").ap()
    dbg1 = nc.dram_tensor("dbg1", [P, C, T], F32, kind="ExternalOutput").ap()
    dbg2 = nc.dram_tensor("dbg2", [P, C, T], F32, kind="ExternalOutput").ap()
    rsp1 = nc.dram_tensor("rsp1", [P, C, T], F32, kind="Internal").ap()
    rsp2 = nc.dram_tensor("rsp2", [P, C, T], F32, kind="Internal").ap()

    c_ones32 = nc.alloc_sbuf_tensor("c_ones32", [1, P], F32)
    c_onesb = nc.alloc_sbuf_tensor("c_onesb", [P, 1], BF16)
    c_ident = nc.alloc_sbuf_tensor("c_ident", [P, P], BF16)
    c_mask = nc.alloc_sbuf_tensor("c_mask", [P, P], BF16)
    c_mur = nc.alloc_sbuf_tensor("c_mur", [P, C], F32)
    c_muk = nc.alloc_sbuf_tensor("c_muk", [P, C], F32)
    c_bpre = nc.alloc_sbuf_tensor("c_bpre", [P, C], F32)
    c_bproj = nc.alloc_sbuf_tensor("c_bproj", [P, C], F32)
    c_s1 = nc.alloc_sbuf_tensor("c_s1", [P, 32], F32)
    c_s2 = nc.alloc_sbuf_tensor("c_s2", [P, 32], F32)
    c_sh1 = nc.alloc_sbuf_tensor("c_sh1", [P, 32], F32)
    c_sh2 = nc.alloc_sbuf_tensor("c_sh2", [P, 32], F32)
    c_s3 = nc.alloc_sbuf_tensor("c_s3", [P, C], F32)
    c_pi = nc.alloc_sbuf_tensor("c_pi", [1, 1], F32)
    c_eps = nc.alloc_sbuf_tensor("c_eps", [1, 1], F32)
    pi32 = nc.alloc_sbuf_tensor("pi32", [P, 1], F32)
    statv = nc.alloc_sbuf_tensor("statv", [P, 16], F32)

    Yt = nc.alloc_sbuf_tensor("Yt", [P, C * (T + 1)], BF16)
    Y = Yt[:].rearrange("p (c t) -> p c t", c=C)

    ring_t = nc.alloc_sbuf_tensor("ring", [P, 6144], BF16)

    SA_BYTES = 32768
    arenaS_t = nc.alloc_sbuf_tensor("arenaS", [P, SA_BYTES // 4], F32)
    AD_BYTES = max(C * T * 4 + 6 * T + C * T * 2,
                   6 * C * T + 16384 + 4 * T + 8192,
                   128 * HTOK)
    arenaD_t = nc.alloc_sbuf_tensor("arenaD", [P, AD_BYTES // 4], F32)

    def view(arena_t, byte_off, shape, dt):
        elt = 2 if dt == BF16 else 4
        assert byte_off % elt == 0
        ncols = 1
        for s in shape[1:]:
            ncols *= s
        hh = arena_t.bitcast(dt) if dt == BF16 else arena_t
        ap = hh[:][:, byte_off // elt: byte_off // elt + ncols]
        if len(shape) == 3:
            ap = ap.rearrange("p (a b) -> p a b", a=shape[1])
        elif len(shape) == 4:
            ap = ap.rearrange("p (a b c) -> p a b c", a=shape[1], b=shape[2])
        return ap

    t1_v = view(arenaS_t, 0, [P, 512], F32)
    t2_v = view(arenaS_t, 2048, [P, 2, 512], F32)
    rb_v = view(arenaS_t, 6144, [P, 2, 512], BF16)
    rsq_v = view(arenaS_t, 8192, [P, 2, 512], BF16)
    st_mean = view(arenaS_t, 10240, [P, 512], F32)[0:1, :]
    st_tmp = view(arenaS_t, 12288, [P, 512], F32)[0:1, :]
    st_rstd = view(arenaS_t, 14336, [P, 512], F32)[0:1, :]
    d_bf = view(arenaS_t, 16384, [P, 512], BF16)
    S_sb = view(arenaS_t, 0, [P, 2, T], F32)
    Pb = view(arenaS_t, 8 * T, [P, 2, T], BF16)
    PT_sb = view(arenaS_t, 12 * T, [P, 2, T // 128, P], BF16)
    ksb_v = view(arenaS_t, 0, [P, 2, 512], F32)
    tv_v = view(arenaS_t, 4096, [P, 2, 512], F32)
    sa_v = view(arenaS_t, 17408, [P, 2, HTOK], BF16)
    rw3_v = view(arenaS_t, 17408 + 4 * HTOK, [P, 2, HTOK], F32)

    R = view(arenaD_t, 0, [P, C, T], F32)
    a0 = C * T * 4
    Qp2 = view(arenaD_t, a0 + 0, [P, 2, T], BF16)
    Kp2 = view(arenaD_t, a0 + 4 * T, [P, 2, T], BF16)
    Vf2 = view(arenaD_t, a0 + 8 * T, [P, 2, T], BF16)
    VT2 = view(arenaD_t, a0 + 8 * T, [P, 2, T // 128, P], BF16)
    O_all = view(arenaD_t, a0 + 12 * T, [P, C, T], BF16)
    xr_v = view(arenaD_t, a0 + 0, [P, C, 512], BF16)
    xk_v = view(arenaD_t, a0 + C * 1024, [P, C, 512], BF16)
    r_all = view(arenaD_t, a0 + C * 2048, [P, C, 512], F32)
    kact_v = view(arenaD_t, a0 + C * 4096, [P, C, 512], BF16)
    pre_v = view(arenaD_t, 0, [P, C, T], BF16)
    K_all = view(arenaD_t, 2 * C * T, [P, C, T], BF16)
    VT_all = view(arenaD_t, 4 * C * T, [P, C, T // 128, P], BF16)
    bB = 6 * C * T
    VfB = view(arenaD_t, bB, [P, T], BF16)
    Qg_v = view(arenaD_t, bB, [P, 2, C, 512], BF16)
    Oq_v = view(arenaD_t, bB + 16384, [P, 2, C, 128], BF16)
    lg_v = view(arenaD_t, bB + 20480, [P, 2, T], BF16)
    rbuf = view(arenaD_t, bB + 20480 + 4 * T, [P, 2, 128], F32)
    Rt512 = view(arenaD_t, 0, [P, 2, C, 512], F32)
    h_v = view(arenaD_t, 0, [P, 32, HTOK], BF16)
    h1g_v = view(arenaD_t, 64 * HTOK, [P, 32, HTOK], BF16)
    hg2_v = h_v

    big = nc.alloc_psum_tensor("big", [P, 2048], F32)[:]
    tp = nc.alloc_psum_tensor("tp", [P, 1024], BF16)[:]
    pv = nc.alloc_psum_tensor("pv", [P, 512], F32)[:]
    qp_h = nc.alloc_psum_tensor("qp", [P, 512], F32)
    qp = qp_h[:]
    qp_bf = qp_h.bitcast(BF16)[:]
    pj = nc.alloc_psum_tensor("pj", [P, 512], F32)[:]

    b = B(nc)
    sSY = b.sem("sSY")
    sGP = b.sem("sGP")
    sW = [b.sem(f"sW{i}") for i in range(3)]
    sRB = [b.sem("sRB0"), b.sem("sRB1")]
    sRW = [b.sem("sRW0"), b.sem("sRW1")]
    sRT = [b.sem("sRT0"), b.sem("sRT1")]
    sSP = b.sem("sSP")
    sTE = b.sem("sTE")
    sDV = b.sem("sDV")
    sAC = b.sem("sAC")
    sPL = b.sem("sPL")

    bz_big = Busy(b)
    bz_bigLo = Busy(b)
    bz_bigHi = Busy(b)
    bz_bigA = Busy(b)
    bz_bigB = Busy(b)
    bz_tp = Busy(b)
    bz_pv = Busy(b)
    bz_qp = Busy(b)
    bz_pj = Busy(b)

    def mm(out, lh, rh, first, last, inc=None):
        return b.op("tensor",
                    lambda t, _o=out, _l=lh, _r=rh, _f=first, _s=last:
                    t.matmul(_o, _l, _r, start=_f, stop=_s), inc=inc)

    def gdma(out, in_, wait_list=()):
        for (s, n) in wait_list:
            b.wait("gpsimd", s, n)
        return b.op("gpsimd", lambda g, _o=out, _i=in_: g.dma_start(out=_o, in_=_i),
                    inc=sGP, by=16)

    def sdma(out, in_, wait_list=()):
        for (s, n) in wait_list:
            b.wait("sync", s, n)
        return b.op("sync", lambda sy, _o=out, _i=in_: sy.dma_start(out=_o, in_=_i),
                    inc=sSY, by=16)

    NSLOT = 3
    slotA = [view(ring_t, i * 4096, [P, 8, 256], BF16) for i in range(NSLOT)]
    slot_free = [[(sTE, 0)] for _ in range(NSLOT)]
    slot_loaded = [0] * NSLOT
    slot_rr = [0]

    def load_slot(src_ap, width=256, dst4=False):
        i = slot_rr[0] % NSLOT
        slot_rr[0] += 1
        if isinstance(src_ap, list):
            n = None
            for j, sap in enumerate(src_ap):
                for (s_, n_) in (slot_free[i] if j == 0 else ()):
                    b.wait("gpsimd", s_, n_)
                n = b.op("gpsimd",
                         lambda g, _o=slotA[i][:, :, 128 * j: 128 * j + 128],
                         _i=sap: g.dma_start(out=_o, in_=_i), inc=sW[i], by=16)
            slot_loaded[i] = n
            return i
        dst = slotA[i] if width == 256 else slotA[i][:, :, 0:width]
        for (s_, n_) in slot_free[i]:
            b.wait("gpsimd", s_, n_)
        n = b.op("gpsimd", lambda g, _o=dst, _i=src_ap:
                 g.dma_start(out=_o, in_=_i), inc=sW[i], by=16)
        slot_loaded[i] = n
        return i

    def use_slot(i):
        b.wait("tensor", sW[i], slot_loaded[i])

    for dst, src in ((c_ones32, ones32D), (c_mur, murD), (c_muk, mukD),
                     (c_bpre, bpreD), (c_bproj, bprojD), (c_s1, s1D), (c_s2, s2D),
                     (c_sh1, sh1D), (c_sh2, sh2D), (c_s3, s3D), (c_pi, piD),
                     (c_eps, epsvD)):
        sdma(dst[:], src)
    n_const_sy = sSY.n
    gdma(c_onesb[:], onesbD)
    gdma(c_ident[:], identD)
    gdma(c_mask[:], maskD)
    n_const_gp = sGP.n
    ld_x = sdma(R, xt.rearrange("(c p) t -> p c t", p=P))
    b.wait("tensor", sSY, n_const_sy)
    mm(qp[:, 0:1], c_ones32[:], c_pi[:], True, True, inc=sTE)
    b.wait("vector", sTE)
    b.op("vector", lambda v: v.tensor_scalar(pi32[:], qp[:, 0:1], 1.0 / 32.0, None,
                                             ALU.mult), inc=sDV)
    bz_qp.set(sDV)
    b.op("vector", lambda v: v.memset(Y[:, :, 0:1], 0.0), inc=sDV)

    def layer_norm(src3, update_R=False, src_ready=None, src_wait=()):
        for s_ in src_wait:
            b.wait("vector", s_)
            b.wait("scalar", s_)
        t2free = [(sPL, 0), (sPL, 0)]
        for tt in range(NT):
            sl = slice(512 * tt, 512 * tt + 512)
            if src_ready is not None:
                src_ready(tt)
            bz_qp.wait("tensor")
            bz_pv.wait("tensor")
            te_c = [0, 0]
            for c in range(C):
                par = c & 1
                if te_c[par]:
                    b.wait("vector", sTE, te_c[par])
                    b.wait("scalar", sTE, te_c[par])
                dv = b.op("vector", lambda v, _c=c, _p=par, _sl=sl:
                          v.tensor_copy(rb_v[:, _p, :], src3(_c, _sl)), inc=sDV)
                ac = b.op("scalar", lambda s, _c=c, _p=par, _sl=sl:
                          s.square(rsq_v[:, _p, :], src3(_c, _sl)), inc=sAC)
                b.wait("tensor", sDV, dv)
                b.wait("tensor", sAC, ac)
                mm(qp[0:1, 0:512], c_onesb[:], rb_v[:, par, :], c == 0, c == C - 1)
                te_c[par] = mm(pv[0:1, 0:512], c_onesb[:], rsq_v[:, par, :],
                               c == 0, c == C - 1, inc=sTE)
            te_stats = sTE.n
            b.wait("vector", sTE, te_stats)
            dv_mean = b.op("vector", lambda v: v.tensor_scalar(
                st_mean, qp[0:1, 0:512], 1.0 / 1024.0, None, ALU.mult), inc=sDV)
            b.wait("scalar", sDV, dv_mean)
            ac_m2 = b.op("scalar", lambda s: s.square(st_tmp, st_mean), inc=sAC)
            b.wait("vector", sAC, ac_m2)
            dv_var = b.op("vector", lambda v: v.scalar_tensor_tensor(
                st_tmp, pv[0:1, 0:512], 1.0 / 1024.0, st_tmp,
                ALU.mult, ALU.subtract), inc=sDV)
            b.wait("scalar", sDV, dv_var)
            ac_sd = b.op("scalar", lambda s: s.activation(
                st_tmp, st_tmp, AF.Sqrt, bias=c_eps[:], scale=1.0), inc=sAC)
            b.wait("vector", sAC, ac_sd)
            dv_rstd = b.op("vector", lambda v: v.reciprocal(st_rstd, st_tmp), inc=sDV)
            bz_qp.set(sDV, dv_mean)
            bz_pv.set(sDV, dv_var)
            b.wait("tensor", sDV, dv_rstd)
            bz_big.wait("tensor")
            bz_bigLo.wait("tensor")
            bz_bigHi.wait("tensor")
            mm(big[:, 0:512], c_ones32[:], st_mean, True, True)
            te_bc = mm(big[:, 512:1024], c_ones32[:], st_rstd, True, True, inc=sTE)
            b.wait("vector", sTE, te_bc)
            last_dv = None
            for c in range(C):
                b.op("vector", lambda v, _c=c, _sl=sl: v.tensor_tensor(
                    t1_v, src3(_c, _sl), big[:, 0:512], ALU.subtract))
                if update_R:
                    par = c & 1
                    b.wait("vector", *t2free[par])
                    dvn = b.op("vector", lambda v, _p=par: v.tensor_tensor(
                        t2_v[:, _p, :], t1_v, big[:, 512:1024], ALU.mult), inc=sDV)
                    b.wait("gpsimd", sDV, dvn)
                    b.op("gpsimd", lambda g, _c=c, _p=par, _sl=sl: g.tensor_copy(
                        Y[:, _c, 1 + _sl.start: 1 + _sl.stop], t2_v[:, _p, :]))
                    pl = b.op("gpsimd", lambda g, _c=c, _p=par, _sl=sl: g.tensor_tensor(
                        R[:, _c, _sl], R[:, _c, _sl], t2_v[:, _p, :], ALU.add), inc=sPL)
                    t2free[par] = (sPL, pl)
                    last_dv = dvn
                else:
                    last_dv = b.op("vector", lambda v, _c=c, _sl=sl: v.tensor_tensor(
                        Y[:, _c, 1 + _sl.start: 1 + _sl.stop], t1_v,
                        big[:, 512:1024], ALU.mult), inc=sDV)
            if update_R:
                bz_big.set(sPL)
                bz_bigLo.set(sPL)
            else:
                bz_big.set(sDV, last_dv)
                bz_bigLo.set(sDV, last_dv)

    def R_src(c, sl):
        return R[:, c, sl]

    b.wait("vector", sSY, ld_x)
    b.wait("scalar", sSY, ld_x)
    b.wait("gpsimd", sGP, n_const_gp)
    if upto == 0:
        b.wait("sync", sSY, ld_x)
        sdma(y, R)
        b.wait("sync", sSY)
        b.emit()
        return nc
    layer_norm(R_src)
    if upto == 10:
        b.wait("gpsimd", sDV)
        b.wait("gpsimd", sPL)
        gdma(y, Y[:, :, 1:T + 1])
        b.wait("gpsimd", sGP)
        b.emit()
        return nc

    def wstream(wsrc, oc_list, tiles, rhs_fn, evac, N=512, wwidth=256, dst4=False):
        si = None
        nexts = list(oc_list)
        si = load_slot(wsrc(nexts[0]), wwidth, dst4)
        for i, oce in enumerate(nexts):
            nsi = (load_slot(wsrc(nexts[i + 1]), wwidth, dst4)
                   if i + 1 < len(nexts) else None)
            use_slot(si)
            te = None
            for sub in ([0, 128] if wwidth == 256 else [0]):
                oc = oce if wwidth == 128 else oce * 2 + (sub // 128)
                for ti in tiles:
                    ppsel = (qp, bz_qp) if (oc & 1) == 0 else (pj, bz_pj)
                    pp, bz = ppsel
                    bz.wait("tensor")
                    for kc in range(C):
                        te = mm(pp[:, 0:N], slotA[si][:, kc, sub: sub + 128],
                                rhs_fn(kc, ti), kc == 0, kc == C - 1,
                                inc=(sTE if kc == C - 1 else None))
                    edge = evac(oc, ti, pp, te)
                    bz.edges = [edge]
            slot_free[si] = [(sTE, te)]
            si = nsi

    def attn_score_softmax(it, qt, Q_ap, K_ap, logits_par, causal, fe):
        par = it & 1
        nk = qt + 1 if causal else NQ
        kl = nk * 128
        nkt = (kl + 511) // 512
        te_s = None
        te_lo = None
        for kt in range(nkt):
            if kt < 2:
                bz_bigLo.wait("tensor")
            else:
                bz_bigHi.wait("tensor")
            n = min(512, kl - kt * 512)
            ksl = slice(kt * 512, kt * 512 + n)
            diag_here = causal and (qt * 128 >= kt * 512) and (qt * 128 < kt * 512 + n)
            lg_here = logits_par is not None
            last_plain = not (diag_here or lg_here)
            lo_end = (kt == 1 or kt == nkt - 1) and kl > 1024 and kt < 2
            te_s = mm(big[:, ksl], Q_ap, K_ap[:, ksl], True, last_plain,
                      inc=(sTE if (last_plain and (kt == nkt - 1 or lo_end))
                           else None))
            if lg_here:
                te_s = mm(big[:, ksl], c_ident[:], lg_v[:, logits_par, ksl],
                          False, not diag_here,
                          inc=(sTE if ((not diag_here) and (kt == nkt - 1 or lo_end))
                               else None))
            if diag_here:
                dsl = slice(qt * 128, qt * 128 + 128)
                te_s = mm(big[:, dsl], c_ident[:], c_mask[:], False, True,
                          inc=(sTE if (kt == nkt - 1 or lo_end) else None))
            if lo_end:
                te_lo = te_s
        base = par * 8
        mxA = statv[:, base: base + 1]
        mxB = statv[:, base + 1: base + 2]
        nm = statv[:, base + 2: base + 3]
        zz = statv[:, base + 3: base + 4]
        rz = statv[:, base + 4: base + 5]
        b.wait("vector", *fe["Sfree"][par])
        if kl > 1024:
            b.wait("vector", sTE, te_lo)
            dv_lo = b.op("vector", lambda v, _p=par, _m=mxA: v.tensor_scalar(
                S_sb[:, _p, 0:1024], big[:, 0:1024], 0.0, None, ALU.add, ALU.max,
                accum_out=_m), inc=sDV)
            bz_bigLo.set(sDV, dv_lo)
            b.wait("vector", sTE, te_s)
            dv_evac = b.op("vector", lambda v, _p=par, _k=kl, _m=mxB: v.tensor_scalar(
                S_sb[:, _p, 1024:_k], big[:, 1024:_k], 0.0, None, ALU.add, ALU.max,
                accum_out=_m), inc=sDV)
            bz_bigHi.set(sDV, dv_evac)
            b.wait("vector", sDV, dv_evac)
            b.op("vector", lambda v, _a=mxA, _b2=mxB: v.tensor_max(_a, _a, _b2),
                 inc=sDV)
            b.wait("vector", sDV)
            dv_nm = b.op("vector", lambda v, _m=mxA, _n=nm: v.tensor_scalar(
                _n, _m, -1.0, None, ALU.mult), inc=sDV)
        else:
            b.wait("vector", sTE, te_s)
            dv_evac = b.op("vector", lambda v, _p=par, _k=kl, _m=mxA: v.tensor_scalar(
                S_sb[:, _p, 0:_k], big[:, 0:_k], 0.0, None, ALU.add, ALU.max,
                accum_out=_m), inc=sDV)
            bz_bigLo.set(sDV, dv_evac)
            bz_bigHi.set(sDV, dv_evac)
            b.wait("vector", sDV, dv_evac)
            dv_nm = b.op("vector", lambda v, _m=mxA, _n=nm: v.tensor_scalar(
                _n, _m, -1.0, None, ALU.mult), inc=sDV)
        b.wait("scalar", sDV, dv_nm)
        b.wait("scalar", *fe["Pfree"][par])
        ac_exp = b.op("scalar", lambda s, _p=par, _k=kl, _n=nm, _z=zz: s.activation(
            Pb[:, _p, 0:_k], S_sb[:, _p, 0:_k], AF.Exp, bias=_n, scale=1.0,
            accum_out=_z), inc=sAC)
        fe["Sfree"][par] = (sAC, ac_exp)
        return ac_exp

    def attn_pv(it, qt, h, ac_exp, VT_ap, O_dst, causal, fe):
        par = it & 1
        nk = qt + 1 if causal else NQ
        zz = statv[:, par * 8 + 3: par * 8 + 4]
        rz = statv[:, par * 8 + 4: par * 8 + 5]
        b.wait("vector", sAC, ac_exp)
        dv_rz = b.op("vector", lambda v, _z=zz, _r=rz: v.reciprocal(_r, _z), inc=sDV)
        b.wait("tensor", sAC, ac_exp)
        pt_edges = []
        for w in range((nk + 7) // 8):
            wn = min(8, nk - w * 8)
            wi = wave_ctr[0] & 1
            wave_ctr[0] += 1
            bank = tp if wi == 0 else qp_bf
            bz_bank = bz_tp if wi == 0 else bz_qp
            bz_bank.wait("tensor")
            te_w = None
            for j in range(wn):
                kc = w * 8 + j
                te_w = b.op("tensor", lambda t, _j=j, _kc=kc, _p=par, _b=bank:
                            t.transpose(_b[:, _j * 128: _j * 128 + 128],
                                        Pb[:, _p, _kc * 128: _kc * 128 + 128],
                                        c_ident[:]),
                            inc=(sTE if j == wn - 1 else None))
            eng = "vector" if wi == 0 else "scalar"
            esem = sDV if wi == 0 else sAC
            b.wait(eng, sTE, te_w)
            for (s_, n_) in [fe["PTfree"][par]]:
                b.wait(eng, s_, n_)
            if wi == 0:
                ac_pt = b.op(eng, lambda e, _w=w, _wn=wn, _p=par, _b=bank:
                             e.tensor_copy(PT_sb[:, _p, _w * 8: _w * 8 + _wn, :],
                                           _b[:, 0: _wn * 128]), inc=esem)
            else:
                ac_pt = b.op(eng, lambda e, _w=w, _wn=wn, _p=par, _b=bank:
                             e.copy(PT_sb[:, _p, _w * 8: _w * 8 + _wn, :],
                                    _b[:, 0: _wn * 128]), inc=esem)
            bz_bank.set(esem, ac_pt)
            pt_edges.append((esem, ac_pt))
        fe["Pfree"][par] = (sTE, sTE.n)
        for (s_, n_) in pt_edges:
            b.wait("tensor", s_, n_)
        if h == 0:
            bz_pv.wait("tensor")
        te_pv = None
        for kc in range(nk):
            te_pv = mm(pv[:, 64 * h: 64 * h + 64], PT_sb[:, par, kc, :], VT_ap(kc),
                       kc == 0, kc == nk - 1,
                       inc=(sTE if kc == nk - 1 else None))
        fe["PTfree"][par] = (sTE, te_pv)
        if h == 1:
            rz0 = statv[:, ((it - 1) & 1) * 8 + 4: ((it - 1) & 1) * 8 + 5]
            b.wait("scalar", sTE, te_pv)
            b.wait("scalar", sDV, dv_rz)
            for (s_, n_) in [fe["Ofree"][par]]:
                b.wait("scalar", s_, n_)
            b.op("scalar", lambda s, _d=O_dst, _r=rz0: s.mul(
                _d[:, 0:64], pv[:, 0:64], _r))
            ac_o = b.op("scalar", lambda s, _d=O_dst, _r=rz: s.mul(
                _d[:, 64:128], pv[:, 64:128], _r), inc=sAC)
            bz_pv.set(sAC, ac_o)
            wi = wave_ctr[0] & 1
            wave_ctr[0] += 1
            bank = tp if wi == 0 else qp_bf
            bz_bank = bz_tp if wi == 0 else bz_qp
            bz_bank.wait("tensor")
            b.wait("tensor", sAC, ac_o)
            te_ot = b.op("tensor", lambda t, _d=O_dst, _b=bank: t.transpose(
                _b[:, 0:128], _d[:, 0:128], c_ident[:]), inc=sTE)
            b.wait("vector", sTE, te_ot)
            dv_o = b.op("vector", lambda v, _d=O_dst, _b=bank: v.tensor_copy(
                _d[:, 0:128], _b[:, 0:128]), inc=sDV)
            bz_bank.set(sDV, dv_o)

    def fresh_fe():
        return {"Sfree": [(sDV, 0), (sDV, 0)],
                "Pfree": [(sDV, 0), (sDV, 0)],
                "PTfree": [(sDV, 0), (sDV, 0)],
                "Ofree": [(sDV, 0), (sDV, 0)]}

    def transpose_to(src_fm, dst_vt):
        for w in range((NQ + 7) // 8):
            wn = min(8, NQ - w * 8)
            bz_tp.wait("tensor")
            te_w = None
            for j in range(wn):
                tc = w * 8 + j
                te_w = b.op("tensor", lambda t, _j=j, _tc=tc: t.transpose(
                    tp[:, _j * 128: _j * 128 + 128],
                    src_fm[:, _tc * 128: _tc * 128 + 128], c_ident[:]),
                    inc=(sTE if j == wn - 1 else None))
            b.wait("scalar", sTE, te_w)
            ac_v = b.op("scalar", lambda s, _w=w, _wn=wn: s.copy(
                dst_vt[:, _w * 8: _w * 8 + _wn, :], tp[:, 0: _wn * 128]), inc=sAC)
            bz_tp.set(sAC, ac_v)

    def ysrc(kc, tt):
        return Y[:, kc, 1 + 512 * tt: 1 + 512 * tt + 512]

    wave_ctr = [0]

    def l0_attn(dump_o=False):
        fe = fresh_fe()
        it_ctr = [0]
        pend = []
        qkvfree = [(sTE, 0), (sTE, 0)]
        b.wait("tensor", sDV)

        def qkv_items(hp):
            par = hp & 1
            st = {}
            items = []

            def mk_tile(part, tt, first):
                def f(_part=part, _tt=tt, _first=first, _hp=hp, _par=par):
                    if _first:
                        if _part == 0:
                            st["si"] = load_slot(
                                wqkv0g[:, _hp, 0:256].rearrange(
                                    "(c p) m -> p c m", p=P))
                        else:
                            st["si"] = load_slot(
                                wqkv0g[:, _hp, 256:384].rearrange(
                                    "(c p) m -> p c m", p=P), width=128)
                    si = st["si"]
                    use_slot(si)
                    sub = 128 if _part == 1 else 0
                    bz_pj.wait("tensor")
                    te = None
                    for kc in range(C):
                        te = mm(pj[:, 0:512], slotA[si][:, kc, sub: sub + 128],
                                Y[:, kc, 1 + 512 * _tt: 1 + 512 * _tt + 512],
                                kc == 0, kc == C - 1,
                                inc=(sTE if kc == C - 1 else None))
                    b.wait("scalar", sTE, te)
                    for (s_, n_) in [qkvfree[_par]]:
                        b.wait("scalar", s_, n_)
                    if _part == 0:
                        ac = b.op("scalar", lambda s, _p=_par, _t=_tt: s.mul(
                            Qp2[:, _p, 512 * _t: 512 * _t + 512], pj[:, 0:512],
                            0.125), inc=sAC)
                    elif _part == 1:
                        ac = b.op("scalar", lambda s, _p=_par, _t=_tt: s.copy(
                            Kp2[:, _p, 512 * _t: 512 * _t + 512], pj[:, 0:512]),
                            inc=sAC)
                    else:
                        ac = b.op("scalar", lambda s, _p=_par, _t=_tt: s.copy(
                            Vf2[:, _p, 512 * _t: 512 * _t + 512], pj[:, 0:512]),
                            inc=sAC)
                    bz_pj.set(sAC, ac)
                    slot_free[si] = [(sTE, te)]
                return f

            for tt in range(NT):
                items.append(mk_tile(0, tt, tt == 0))
            for tt in range(NT):
                items.append(mk_tile(1, tt, False))
            for tt in range(NT):
                items.append(mk_tile(2, tt, tt == 0))

            def mk_wave(w):
                def f(_w=w, _par=par):
                    wn = min(8, NQ - _w * 8)
                    wi = wave_ctr[0] & 1
                    wave_ctr[0] += 1
                    bank = tp if wi == 0 else qp_bf
                    bz_bank = bz_tp if wi == 0 else bz_qp
                    b.wait("tensor", sAC)
                    bz_bank.wait("tensor")
                    te_w = None
                    for j in range(wn):
                        tc = _w * 8 + j
                        te_w = b.op("tensor", lambda t, _j=j, _tc=tc, _b=bank:
                                    t.transpose(
                                        _b[:, _j * 128: _j * 128 + 128],
                                        Vf2[:, _par, _tc * 128: _tc * 128 + 128],
                                        c_ident[:]),
                                    inc=(sTE if j == wn - 1 else None))
                    eng = "vector" if wi == 0 else "scalar"
                    esem = sDV if wi == 0 else sAC
                    b.wait(eng, sTE, te_w)
                    if wi == 0:
                        ac_v = b.op(eng, lambda e, _w2=_w, _wn=wn, _b=bank:
                                    e.tensor_copy(
                                        VT2[:, _par, _w2 * 8: _w2 * 8 + _wn, :],
                                        _b[:, 0: _wn * 128]), inc=esem)
                    else:
                        ac_v = b.op(eng, lambda e, _w2=_w, _wn=wn, _b=bank:
                                    e.copy(
                                        VT2[:, _par, _w2 * 8: _w2 * 8 + _wn, :],
                                        _b[:, 0: _wn * 128]), inc=esem)
                    bz_bank.set(esem, ac_v)
                    b.wait("tensor", esem, ac_v)
                return f

            for w in range((NQ + 7) // 8):
                items.append(mk_wave(w))
            return items

        for f in qkv_items(0):
            f()
        for hp in range(NPAIR):
            par = hp & 1
            fillers = qkv_items(hp + 1) if hp + 1 < NPAIR else []
            ui = 0
            for qt in range(NQ):
                for h in (0, 1):
                    it = it_ctr[0]
                    it_ctr[0] += 1
                    if len(pend) == 2:
                        attn_pv(*pend.pop(0))
                    dvn = attn_score_softmax(
                        it, qt,
                        Q_ap=Qp2[64 * h: 64 * h + 64, par,
                                 qt * 128: qt * 128 + 128],
                        K_ap=Kp2[64 * h: 64 * h + 64, par, :],
                        logits_par=None, causal=True, fe=fe)
                    pend.append((it, qt, h, dvn,
                                 (lambda kc, _h=h, _par=par:
                                  VT2[:, _par, kc, 64 * _h: 64 * _h + 64]),
                                 O_all[:, hp, qt * 128: qt * 128 + 128], True, fe))
                    if fillers and ui % 2 == 1:
                        fillers.pop(0)()
                    ui += 1
            while pend:
                attn_pv(*pend.pop(0))
            while fillers:
                fillers.pop(0)()
            qkvfree[par] = (sTE, sTE.n)

        def wsrc_p0(ocp):
            return wproj0T[:, ocp * 256: ocp * 256 + 256].rearrange(
                "(c p) m -> p c m", p=P)

        def rhs_o(kc, tt):
            return O_all[:, kc, 512 * tt: 512 * tt + 512]

        def evac_p0(oc, tt, pp, te):
            b.wait("vector", sTE, te)
            dv = b.op("vector", lambda v, _oc=oc, _tt=tt, _pp=pp: v.tensor_tensor(
                R[:, _oc, 512 * _tt: 512 * _tt + 512], _pp[:, 0:512],
                R[:, _oc, 512 * _tt: 512 * _tt + 512], ALU.add), inc=sDV)
            return (sDV, dv)

        if not dump_o:
            b.wait("tensor", sDV)
            wstream(wsrc_p0, list(range(4)), list(range(NT)), rhs_o, evac_p0)

    l0_attn(dump_o=(upto == 12))
    if upto == 12:
        b.wait("gpsimd", sDV)
        gdma(y, O_all)
        b.wait("gpsimd", sGP)
        b.emit()
        return nc
    if upto == 1:
        b.wait("sync", sDV)
        sdma(y, R)
        b.wait("sync", sSY)
        b.emit()
        return nc

    layer_norm(R_src, src_wait=(sDV,))

    def l0_ffn():
        for tt in range(NT):
            sl = slice(512 * tt, 512 * tt + 512)
            dv_mix = None
            for c in range(C):
                yc = Y[:, c, 1 + sl.start: 1 + sl.stop]
                yp = Y[:, c, sl.start: sl.stop]
                b.op("vector", lambda v, _a=yc, _b2=yp: v.tensor_tensor(
                    d_bf[:, :], _a, _b2, ALU.subtract))
                b.op("vector", lambda v, _c=c, _y=yp: v.scalar_tensor_tensor(
                    xr_v[:, _c, :], d_bf[:, :], c_mur[:, _c:_c + 1], _y,
                    ALU.mult, ALU.add))
                dv_mix = b.op("vector", lambda v, _c=c, _y=yp: v.scalar_tensor_tensor(
                    xk_v[:, _c, :], d_bf[:, :], c_muk[:, _c:_c + 1], _y,
                    ALU.mult, ALU.add), inc=sDV)
            b.wait("tensor", sDV, dv_mix)

            def rhs_xr(kc, _t):
                return xr_v[:, kc, :]

            def rhs_xk(kc, _t):
                return xk_v[:, kc, :]

            def rhs_kact(kc, _t):
                return kact_v[:, kc, :]

            def evac_r(oc, _t, pp, te):
                b.wait("scalar", sTE, te)
                ac = b.op("scalar", lambda s, _oc=oc, _pp=pp: s.activation(
                    r_all[:, _oc, :], _pp[:, 0:512], AF.Sigmoid), inc=sAC)
                return (sAC, ac)

            kfree = [(sDV, 0), (sDV, 0)]

            def evac_k(oc, _t, pp, te):
                par = oc & 1
                b.wait("scalar", sTE, te)
                b.wait("scalar", *kfree[par])
                ac = b.op("scalar", lambda s, _p=par, _pp=pp: s.copy(
                    ksb_v[:, _p, :], _pp[:, 0:512]), inc=sAC)
                b.wait("vector", sAC, ac)
                dv = b.op("vector", lambda v, _oc=oc, _p=par: v.scalar_tensor_tensor(
                    kact_v[:, _oc, :], ksb_v[:, _p, :], 0.0, ksb_v[:, _p, :],
                    ALU.max, ALU.mult), inc=sDV)
                kfree[par] = (sDV, dv)
                return (sAC, ac)

            tvfree = [(sPL, 0), (sPL, 0)]

            def evac_vv(oc, _t, pp, te):
                par = oc & 1
                b.wait("vector", sTE, te)
                b.wait("vector", *tvfree[par])
                dv = b.op("vector", lambda v, _p=par, _oc=oc, _pp=pp: v.tensor_tensor(
                    tv_v[:, _p, :], _pp[:, 0:512], r_all[:, _oc, :], ALU.mult),
                    inc=sDV)
                b.wait("gpsimd", sDV, dv)
                pl = b.op("gpsimd", lambda g, _oc=oc, _p=par, _sl=sl: g.tensor_tensor(
                    R[:, _oc, _sl], R[:, _oc, _sl], tv_v[:, _p, :], ALU.add), inc=sPL)
                tvfree[par] = (sPL, pl)
                return (sDV, dv)

            def wsrc_w(wT):
                return lambda ocp, _w=wT: _w[:, ocp * 256: ocp * 256 + 256].rearrange(
                    "(c p) m -> p c m", p=P)

            wstream(wsrc_w(wrT), list(range(4)), [0], rhs_xr, evac_r)
            wstream(wsrc_w(wkT), list(range(4)), [0], rhs_xk, evac_k)
            b.wait("tensor", sDV)
            wstream(wsrc_w(wvT), list(range(4)), [0], rhs_kact, evac_vv)

    l0_ffn()
    if upto == 2:
        b.wait("sync", sPL)
        b.wait("sync", sDV)
        sdma(y, R)
        b.wait("sync", sSY)
        b.emit()
        return nc

    layer_norm(R_src, update_R=True, src_wait=(sPL, sDV))
    b.wait("sync", sPL)
    b.wait("sync", sDV)
    sp_r = b.op("sync", lambda sy: sy.dma_start(out=rsp1, in_=R), inc=sSP, by=16)

    def l1_attn():
        b.wait("tensor", sPL)
        def wsrc_pre(ocp):
            return wpre1[:, ocp * 256: ocp * 256 + 256].rearrange(
                "(c p) m -> p c m", p=P)

        def evac_pre(oc, tt, pp, te):
            b.wait("scalar", sTE, te)
            b.wait("scalar", sSY, sp_r)
            ac = b.op("scalar", lambda s, _oc=oc, _tt=tt, _pp=pp: s.activation(
                pre_v[:, _oc, 512 * _tt: 512 * _tt + 512], _pp[:, 0:512], AF.Silu,
                bias=c_bpre[:, _oc:_oc + 1], scale=1.0), inc=sAC)
            return (sAC, ac)

        wstream(wsrc_pre, list(range(4)), list(range(NT)), ysrc, evac_pre)
        for hp in range(NPAIR):
            def wsrc_kv(_e, _hp=hp):
                return wqkv1g[:, _hp, 128:384].rearrange("(c p) m -> p c m", p=P)

            def evac_kv(oc, tt, pp, te):
                b.wait("scalar", sTE, te)
                if oc % 2 == 0:
                    ac = b.op("scalar", lambda s, _tt=tt, _pp=pp, _hp=hp: s.copy(
                        K_all[:, _hp, 512 * _tt: 512 * _tt + 512], _pp[:, 0:512]),
                        inc=sAC)
                else:
                    ac = b.op("scalar", lambda s, _tt=tt, _pp=pp: s.copy(
                        VfB[:, 512 * _tt: 512 * _tt + 512], _pp[:, 0:512]), inc=sAC)
                return (sAC, ac)

            wstream(wsrc_kv, [0], list(range(NT)), ysrc, evac_kv)
            b.wait("tensor", sAC)
            transpose_to(VfB, VT_all[:, hp, :, :])
        b.wait("tensor", sAC)

        fe = fresh_fe()
        it_ctr = [0]
        pend = []
        lfree = [(sTE, 0), (sTE, 0)]
        qgfree = [(sTE, 0), (sTE, 0)]
        oq_done = [(sDV, 0), (sDV, 0)]
        rbfree = [(sDV, 0), (sDV, 0)]
        fillers = []

        def q_item(gn, hpp):
            gpar = gn & 1
            srcs = [wqkv1g[:, 2 * hpp + j, 0:128].rearrange(
                "(c p) m -> p c m", p=P) for j in range(2)]
            si = load_slot(srcs)
            use_slot(si)
            te = None
            for sub in (0, 1):
                oc = 2 * hpp + sub
                bz_pj.wait("tensor")
                for kc in range(C):
                    te = mm(pj[:, 0:512], slotA[si][:, kc, sub * 128: sub * 128 + 128],
                            Y[:, kc, 1 + 512 * gn: 1 + 512 * gn + 512],
                            kc == 0, kc == C - 1,
                            inc=(sTE if kc == C - 1 else None))
                b.wait("scalar", sTE, te)
                b.wait("scalar", *qgfree[gpar])
                ac = b.op("scalar", lambda s, _gp=gpar, _oc=oc: s.mul(
                    Qg_v[:, _gp, _oc, :], pj[:, 0:512], 0.125), inc=sAC)
                bz_pj.set(sAC, ac)
            slot_free[si] = [(sTE, te)]

        def p_item(qt_, ocp):
            opar = qt_ & 1
            qsl_ = slice(qt_ * 128, qt_ * 128 + 128)
            si = load_slot(wproj1[:, ocp * 256: ocp * 256 + 256].rearrange(
                "(c p) m -> p c m", p=P))
            use_slot(si)
            te = None
            for sub in (0, 1):
                oc = 2 * ocp + sub
                par = oc & 1
                b.wait("sync", sSP, sp_r)
                for (s_, n_) in [rbfree[par]]:
                    b.wait("sync", s_, n_)
                ld = b.op("sync", lambda sy, _p=par, _oc=oc, _q=qsl_:
                          sy.dma_start(out=rbuf[:, _p, :], in_=rsp1[:, _oc, _q]),
                          inc=sRB[par], by=16)
                bz_qp.wait("tensor")
                b.wait("tensor", *oq_done[opar])
                for kc in range(C):
                    te = mm(qp[:, 0:128], slotA[si][:, kc, sub * 128: sub * 128 + 128],
                            Oq_v[:, opar, kc, :], kc == 0, kc == C - 1,
                            inc=(sTE if kc == C - 1 else None))
                b.wait("vector", sTE, te)
                b.wait("vector", sRB[par], ld)
                dv = b.op("vector", lambda v, _p=par, _oc=oc:
                          v.scalar_tensor_tensor(rbuf[:, _p, :], qp[:, 0:128],
                                                 c_bproj[:, _oc:_oc + 1],
                                                 rbuf[:, _p, :], ALU.add, ALU.add),
                          inc=sDV)
                bz_qp.set(sDV, dv)
                b.wait("sync", sDV, dv)
                dest = y if upto == 3 else rsp2
                sdma(dest[:, oc, qsl_], rbuf[:, par, :])
                rbfree[par] = (sDV, dv)
            slot_free[si] = [(sTE, te)]

        for hpp in range(4):
            q_item(0, hpp)
        b.wait("tensor", sAC)
        for qt in range(NQ):
            g = qt // 4
            qi = qt % 4
            lg_par = qt & 1
            qsl = slice(qt * 128, qt * 128 + 128)
            bz_bigLo.wait("tensor")
            bz_bigHi.wait("tensor")
            bz_big.wait("tensor")
            te_lg = None
            for kt in range(NT):
                ksl = slice(kt * 512, kt * 512 + 512)
                for kc in range(C):
                    te_lg = mm(big[:, ksl], pre_v[:, kc, qsl], pre_v[:, kc, ksl],
                               kc == 0, kc == C - 1,
                               inc=(sTE if (kc == C - 1 and kt == NT - 1)
                                    else None))
            b.wait("scalar", sTE, te_lg)
            b.wait("scalar", *lfree[lg_par])
            ac_lg = b.op("scalar", lambda s, _p=lg_par: s.activation(
                lg_v[:, _p, :], big[:, 0:T], AF.Copy, bias=0.0,
                scale=pi32[:, 0:1]), inc=sAC)
            bz_big.set(sAC, ac_lg)
            bz_bigLo.set(sAC, ac_lg)
            bz_bigHi.set(sAC, ac_lg)
            b.wait("tensor", sAC, ac_lg)
            for u in range(16):
                hp, h = u // 2, u % 2
                if u == 2:
                    if qt > 0:
                        oq_done[(qt - 1) & 1] = (sDV, sDV.n)
                        for ocp in range(4):
                            fillers.append(lambda _q=qt - 1, _o=ocp: p_item(_q, _o))
                    if qi == 3 and g + 1 < NG:
                        for hpp in range(4):
                            fillers.append(lambda _g=g + 1, _h=hpp: q_item(_g, _h))
                it = it_ctr[0]
                it_ctr[0] += 1
                if len(pend) == 2:
                    attn_pv(*pend.pop(0))
                dvn = attn_score_softmax(
                    it, qt,
                    Q_ap=Qg_v[64 * h: 64 * h + 64, g & 1, hp,
                              qi * 128: qi * 128 + 128],
                    K_ap=K_all[64 * h: 64 * h + 64, hp, :],
                    logits_par=lg_par, causal=False, fe=fe)
                pend.append((it, qt, h, dvn,
                             (lambda kc, _hp=hp, _h=h:
                              VT_all[:, _hp, kc, 64 * _h: 64 * _h + 64]),
                             Oq_v[:, qt & 1, hp, :], False, fe))
                if fillers:
                    fillers.pop(0)()
            while fillers:
                fillers.pop(0)()
            lfree[lg_par] = (sTE, sTE.n)
            if qi == 3:
                qgfree[g & 1] = (sTE, sTE.n)
            if qt > 0:
                fe["Ofree"][(qt - 1) & 1] = (sTE, sTE.n)
        while pend:
            attn_pv(*pend.pop(0))
        oq_done[(NQ - 1) & 1] = (sDV, sDV.n)
        for ocp in range(4):
            p_item(NQ - 1, ocp)

    l1_attn()
    if upto == 16:
        b.wait("sync", sDV)
        sdma(dbg1[:, 0, 0:512], rw3_v[:, 0, 0:512])
        sdma(dbg1[:, 1, 0:512], rw3_v[:, 1, 0:512])
        sdma(dbg1[:, 2:3, 0:512], rsp1[:, 0:1, 0:512])
        b.wait("sync", sSY)
        b.emit()
        return nc
    if upto == 3:
        b.wait("sync", sSY)
        b.emit()
        return nc

    def c0_src_ready(tt):
        par = tt & 1
        if tt == 0:
            b.wait("sync", sSY)
        b.wait("sync", sDV, sDV.n)
        b.wait("sync", sAC, sAC.n)
        ld = b.op("sync", lambda sy, _p=par, _tt=tt: sy.dma_start(
            out=Rt512[:, _p, :, :],
            in_=rsp2[:, :, 512 * _tt: 512 * _tt + 512]), inc=sRT[par], by=16)
        b.wait("vector", sRT[par], ld)
        b.wait("scalar", sRT[par], ld)

    def c0_src(c, sl):
        par = (sl.start // 512) & 1
        return Rt512[:, par, c, :]

    layer_norm(c0_src, update_R=False, src_ready=c0_src_ready)

    def ffn_layer(src3, n_kc, n_oc, wdram, evac):
        nkcg = n_kc // 8
        pend = [(ocp, kcg) for ocp in range(n_oc // 2) for kcg in range(nkcg)]
        si = load_slot(wdram(*pend[0]))
        for idx, (ocp, kcg) in enumerate(pend):
            nsi = load_slot(wdram(*pend[idx + 1])) if idx + 1 < len(pend) else None
            use_slot(si)
            te = None
            for o2 in range(2):
                oc = ocp * 2 + o2
                if kcg == 0:
                    (bz_bigA if o2 == 0 else bz_bigB).wait("tensor")
                for nt in range(NTH):
                    ps = big[:, (o2 * NTH + nt) * 512: (o2 * NTH + nt) * 512 + 512]
                    for kc8 in range(8):
                        kc = kcg * 8 + kc8
                        last = (kcg == nkcg - 1 and kc8 == 7)
                        te = mm(ps, slotA[si][:, kc8, o2 * 128: o2 * 128 + 128],
                                src3(kc, nt), kcg == 0 and kc8 == 0, last,
                                inc=(sTE if (last and o2 == 1 and nt == NTH - 1)
                                     else None))
            if kcg == nkcg - 1:
                for o2 in range(2):
                    oc = ocp * 2 + o2
                    tiles = [big[:, (o2 * NTH + nt) * 512: (o2 * NTH + nt) * 512 + 512]
                             for nt in range(NTH)]
                    edge = evac(oc, tiles, te)
                    (bz_bigA if o2 == 0 else bz_bigB).edges = [edge]
            slot_free[si] = [(sTE, te)]
            si = nsi

    def l1_ffn():
        b.wait("tensor", sDV)
        for hf in range(NHALF):
            h0 = hf * HTOK

            def yh(kc, nt, _h0=h0):
                return Y[:, kc, 1 + _h0 + nt * 512: 1 + _h0 + nt * 512 + 512]

            safree = [(sDV, 0), (sDV, 0)]

            def evac_w1(oc, tiles, te_n):
                par = oc & 1
                b.wait("scalar", sTE, te_n)
                b.wait("scalar", *safree[par])
                ac = None
                for nt in range(NTH):
                    ac = b.op("scalar", lambda s, _oc=oc, _p=par, _nt=nt,
                              _ps=tiles[nt]: s.activation(
                                  sa_v[:, _p, _nt * 512: _nt * 512 + 512], _ps,
                                  AF.Silu, bias=0.0, scale=c_s1[:, _oc:_oc + 1]),
                              inc=sAC)
                return (sAC, ac)

            def evac_w2(oc, tiles, te_n):
                par = oc & 1
                b.wait("vector", sTE, te_n)
                b.wait("vector", sAC)
                dv = None
                for nt in range(NTH):
                    dv = b.op("vector", lambda v, _oc=oc, _p=par, _nt=nt,
                              _ps=tiles[nt]: v.scalar_tensor_tensor(
                                  h_v[:, _oc, _nt * 512: _nt * 512 + 512], _ps,
                                  c_s2[:, _oc:_oc + 1],
                                  sa_v[:, _p, _nt * 512: _nt * 512 + 512],
                                  ALU.mult, ALU.mult), inc=sDV)
                safree[par] = (sDV, dv)
                return (sDV, dv)

            for ocp in range(16):
                for (wd, evac) in ((w1T, evac_w1), (w2T, evac_w2)):
                    si = load_slot(wd[:, ocp * 256:(ocp + 1) * 256].rearrange(
                        "(k p) o -> p k o", p=P))
                    use_slot(si)
                    te = None
                    for o2 in range(2):
                        oc = ocp * 2 + o2
                        (bz_bigA if o2 == 0 else bz_bigB).wait("tensor")
                        for nt in range(NTH):
                            ps = big[:, (o2 * NTH + nt) * 512:
                                     (o2 * NTH + nt) * 512 + 512]
                            for kc in range(C):
                                te = mm(ps, slotA[si][:, kc, o2 * 128: o2 * 128 + 128],
                                        yh(kc, nt), kc == 0, kc == C - 1,
                                        inc=(sTE if (kc == C - 1 and o2 == 1
                                                     and nt == NTH - 1) else None))
                    for o2 in range(2):
                        oc = ocp * 2 + o2
                        tiles = [big[:, (o2 * NTH + nt) * 512:
                                     (o2 * NTH + nt) * 512 + 512]
                                 for nt in range(NTH)]
                        edge = evac(oc, tiles, te)
                        (bz_bigA if o2 == 0 else bz_bigB).edges = [edge]
                    slot_free[si] = [(sTE, te)]

            def hsrc(kc, nt):
                return h_v[:, kc, nt * 512: nt * 512 + 512]

            def evac_h1(oc, tiles, te_n):
                b.wait("scalar", sTE, te_n)
                ac = None
                for nt in range(NTH):
                    ac = b.op("scalar", lambda s, _oc=oc, _nt=nt, _ps=tiles[nt]:
                              s.activation(h1g_v[:, _oc, _nt * 512: _nt * 512 + 512],
                                           _ps, AF.Gelu, bias=0.0,
                                           scale=c_sh1[:, _oc:_oc + 1]), inc=sAC)
                return (sAC, ac)

            b.wait("tensor", sDV)
            ffn_layer(hsrc, 32, 32,
                      lambda ocp, kcg: h1T[kcg * 1024:(kcg + 1) * 1024,
                                           ocp * 256:(ocp + 1) * 256].rearrange(
                                               "(k p) o -> p k o", p=P), evac_h1)

            def h1src(kc, nt):
                return h1g_v[:, kc, nt * 512: nt * 512 + 512]

            def evac_h2(oc, tiles, te_n):
                b.wait("scalar", sTE, te_n)
                ac = None
                for nt in range(NTH):
                    ac = b.op("scalar", lambda s, _oc=oc, _nt=nt, _ps=tiles[nt]:
                              s.activation(hg2_v[:, _oc, _nt * 512: _nt * 512 + 512],
                                           _ps, AF.Gelu, bias=0.0,
                                           scale=c_sh2[:, _oc:_oc + 1]), inc=sAC)
                return (sAC, ac)

            b.wait("tensor", sAC)
            ffn_layer(h1src, 32, 32,
                      lambda ocp, kcg: h2T[kcg * 1024:(kcg + 1) * 1024,
                                           ocp * 256:(ocp + 1) * 256].rearrange(
                                               "(k p) o -> p k o", p=P), evac_h2)

            def h2src(kc, nt):
                return hg2_v[:, kc, nt * 512: nt * 512 + 512]

            rwfree = [(sSY, 0), (sSY, 0)]

            def evac_w3(oc, tiles, te_n, _h0=h0):
                par = oc & 1
                for (s_, n_) in [rwfree[par]]:
                    b.wait("sync", s_, n_)
                ld = b.op("sync", lambda sy, _p=par, _oc=oc:
                          sy.dma_start(out=rw3_v[:, _p, 0:HTOK],
                                       in_=rsp2[:, _oc, _h0: _h0 + HTOK]),
                          inc=sRW[par], by=16)
                b.wait("vector", sTE, te_n)
                b.wait("vector", sRW[par], ld)
                dv = None
                for nt in range(NTH):
                    dv = b.op("vector", lambda v, _oc=oc, _p=par, _nt=nt,
                              _ps=tiles[nt]: v.scalar_tensor_tensor(
                                  rw3_v[:, _p, _nt * 512: _nt * 512 + 512], _ps,
                                  c_s3[:, _oc:_oc + 1],
                                  rw3_v[:, _p, _nt * 512: _nt * 512 + 512],
                                  ALU.mult, ALU.add), inc=sDV)
                b.wait("sync", sDV, dv)
                sdma(y[:, oc, _h0: _h0 + HTOK], rw3_v[:, par, 0:HTOK])
                rwfree[par] = (sDV, dv)
                return (sDV, dv)

            b.wait("tensor", sAC)
            ffn_layer(h2src, 32, C,
                      lambda ocp, kcg: w3T[kcg * 1024:(kcg + 1) * 1024,
                                           ocp * 256:(ocp + 1) * 256].rearrange(
                                               "(k p) o -> p k o", p=P), evac_w3)

    l1_ffn()
    b.wait("sync", sSY)
    b.wait("gpsimd", sGP)
    b.emit()
    return nc



def _prep_shared(kw):
    f32 = np.float32
    w = kw
    wqkv = np.asarray(w["L0_attn_w_qkv"], f32)
    wq0, wk0, wv0 = wqkv[0:1024], wqkv[1024:2048], wqkv[2048:3072]
    g0 = np.stack([np.concatenate([wq0[128 * p:128 * p + 128],
                                   wk0[128 * p:128 * p + 128],
                                   wv0[128 * p:128 * p + 128]], axis=0)
                   for p in range(8)], axis=0)
    wqkv0g = np.ascontiguousarray(g0.transpose(2, 0, 1))

    Wq1 = np.asarray(w["L1_attn_Wq"], f32)
    Wk1 = np.asarray(w["L1_attn_Wk"], f32)
    Wv1 = np.asarray(w["L1_attn_Wv"], f32)
    g1 = np.stack([np.concatenate([Wq1[:, 128 * p:128 * p + 128],
                                   Wk1[:, 128 * p:128 * p + 128],
                                   Wv1[:, 128 * p:128 * p + 128]], axis=1)
                   for p in range(8)], axis=1)
    wqkv1g = np.ascontiguousarray(g1)

    def colmaj(v, nchunk):
        return np.ascontiguousarray(np.asarray(v, f32).reshape(nchunk, 128).T)

    return {
        "wqkv0g": wqkv0g,
        "wproj0T": np.ascontiguousarray(np.asarray(w["L0_attn_w_proj"], f32).T),
        "wrT": np.ascontiguousarray(np.asarray(w["L0_ffn_W_r"], f32).T),
        "wkT": np.ascontiguousarray(np.asarray(w["L0_ffn_W_k"], f32).T),
        "wvT": np.ascontiguousarray(np.asarray(w["L0_ffn_W_v"], f32).T),
        "mur": colmaj(w["L0_ffn_mu_r"], 8),
        "muk": colmaj(w["L0_ffn_mu_k"], 8),
        "wqkv1g": wqkv1g,
        "wpre1": np.ascontiguousarray(np.asarray(w["L1_attn_Wpre"], f32)),
        "wproj1": np.ascontiguousarray(np.asarray(w["L1_attn_Wproj"], f32)),
        "bpre": colmaj(w["L1_attn_bpre"], 8),
        "bproj": colmaj(w["L1_attn_bproj"], 8),
        "pi": np.asarray(w["L1_attn_pi"], f32).reshape(1, 1),
        "w1T": np.ascontiguousarray(np.asarray(w["L1_ffn_w1_w"], f32).T),
        "w2T": np.ascontiguousarray(np.asarray(w["L1_ffn_w2_w"], f32).T),
        "h1T": np.ascontiguousarray(np.asarray(w["L1_ffn_h1_w"], f32).T),
        "h2T": np.ascontiguousarray(np.asarray(w["L1_ffn_h2_w"], f32).T),
        "w3T": np.ascontiguousarray(np.asarray(w["L1_ffn_w3_w"], f32).T),
        "s1": colmaj(w["L1_ffn_w1_s"], 32),
        "s2": colmaj(w["L1_ffn_w2_s"], 32),
        "sh1": colmaj(w["L1_ffn_h1_s"], 32),
        "sh2": colmaj(w["L1_ffn_h2_s"], 32),
        "s3": colmaj(w["L1_ffn_w3_s"], 8),
        "ones32": np.ones((1, 128), f32),
        "epsv": np.full((1, 1), 1e-5, f32),
        "onesb": np.ones((128, 1), f32),
        "ident": np.eye(128, dtype=f32),
        "maskc": np.where(np.arange(128)[None, :] > np.arange(128)[:, None],
                          np.float32(-1e30), np.float32(0.0)).astype(f32),
    }


def kernel(**kw):
    T = 2048
    x = np.asarray(kw["L0_attn_x"], np.float32)
    ncores = x.shape[0]
    shared = _prep_shared(kw)
    nc = build_nc(T=T, upto=4)
    in_maps = []
    for bi in range(ncores):
        m = dict(shared)
        m["xt"] = np.ascontiguousarray(x[bi].T)
        in_maps.append(m)
    res = run_bass_kernel_spmd(nc, in_maps, core_ids=list(range(ncores)))
    out = np.empty((ncores, T, 1024), np.float32)
    for bi, r in enumerate(res.results):
        yv = np.asarray(r["y"])
        out[bi] = yv.transpose(1, 0, 2).reshape(1024, T).T
    return out
